# version 05 (final)
# baseline (speedup 1.0000x reference)
import contextlib
import numpy as np
import ml_dtypes
import concourse.bass as bass
import concourse.mybir as mybir
from concourse.bass_utils import run_bass_kernel_spmd

F32 = mybir.dt.float32
BF16 = mybir.dt.bfloat16
AF = mybir.ActivationFunctionType
ALU = mybir.AluOpType
AX = mybir.AxisListType
PI = 3.141592653589793
BF = ml_dtypes.bfloat16

D = 1024
NCH = 8
GAMMA = 0.96875
ROUTE_SCALE = 2.826
NEXP = 16

C_LNEPS = 0
C_EPS6 = 1
C_G8 = 2
C_NCONST = 4


class Sem:
    __slots__ = ("h", "n", "b")

    def __init__(self, h):
        self.h = h
        self.n = 0
        self.b = 0

    def inc(self, inst, k=1):
        inst.then_inc(self.h, k)
        self.n += k
        return inst

    def wait(self, eng, v):
        eng.wait_ge(self.h, self.b + v)
class Ctx:
    def __init__(self, nc, es, T, S):
        self.nc, self.es, self.T, self.S = nc, es, T, S
        self.block = None
        self.dram = {}
        self.glob = {}
        self.pid = 0
        self._sems = {}
        self.bar = None
        self.st = None
        self.weff_ready = False

    def sem(self, name):
        if name not in self._sems:
            self._sems[name] = Sem(self.es.enter_context(self.nc.semaphore(name)))
        return self._sems[name]

    def fresh(self):
        self._fresh = getattr(self, "_fresh", 0) + 1
        return self.sem(f"sw{self._fresh}")

    def vdrain(self, eng, inst):
        s = self.sem("drv")
        s.inc(inst, 1)
        eng.wait_ge(s.h, s.n)
        return inst

    def phase_sems(self, names):
        self.pid += 1
        out = []
        for j in range(len(names)):
            sm = self.sem(f"pool{j}")
            sm.b = sm.n
            out.append(sm)
        return out



def make_tables(S):
    d = D
    half = d // 2
    scale_vec = (np.arange(0, d, 2, dtype=np.float64) + 0.4 * d) / (1.4 * d)
    pos = np.arange(S, dtype=np.float64)
    scale = scale_vec[None, :] ** (pos[:, None] / 512.0)
    inv_freq = 1.0 / 10000.0 ** (np.arange(half, dtype=np.float64) / half)
    sinus = pos[:, None] * inv_freq[None, :]
    t = {}
    t["tq_sin"] = np.ascontiguousarray(np.repeat(np.sin(sinus) * scale, 2, -1).T.astype(BF))
    t["tq_cos"] = np.ascontiguousarray(np.repeat(np.cos(sinus) * scale, 2, -1).T.astype(BF))
    t["tk_sin"] = np.ascontiguousarray(np.repeat(np.sin(sinus) / scale, 2, -1).T.astype(BF))
    t["tk_cos"] = np.ascontiguousarray(np.repeat(np.cos(sinus) / scale, 2, -1).T.astype(BF))
    i = np.arange(128)[:, None]
    j = np.arange(128)[None, :]
    gl = GAMMA ** (j - i).astype(np.float64)
    t["gloc"] = gl.astype(np.float32)
    t["gloc_tri"] = np.where(j >= i, gl, 0.0).astype(np.float32)
    R = np.zeros((128, 128), dtype=BF)
    ii = np.arange(0, 128, 2)
    R[ii + 1, ii] = -1.0
    R[ii, ii + 1] = 1.0
    t["rotm"] = R
    t["ones_bf"] = np.ones((128, 128), dtype=BF)
    t["ident_bf"] = np.eye(128, dtype=BF)
    oh = np.zeros((128, NEXP * 128), dtype=BF)
    for e in range(NEXP):
        oh[e, 128 * e:128 * e + 128] = 1.0
    t["onehots"] = oh
    lt = np.zeros((NEXP, NEXP), dtype=np.float32)
    for a in range(NEXP):
        lt[a, :a] = 1.0
    t["ltmask"] = np.ascontiguousarray(np.tile(lt.reshape(1, -1), (128, 1)))
    cb = np.zeros((128, C_NCONST), dtype=np.float32)
    cb[:, C_LNEPS] = 1e-5
    cb[:, C_EPS6] = 1e-6
    cb[:, C_G8] = 0.125
    t["cbank"] = cb
    return t


def prep_weights(inp):
    f32 = lambda a: np.ascontiguousarray(np.asarray(a), dtype=np.float32)
    w = {}
    w["wA_qkv"] = f32(inp["L0_attn_a_W_qkv"])
    w["bA_qkv"] = np.ascontiguousarray(np.tile(f32(inp["L0_attn_a_b_qkv"])[None, :], (128, 1)))
    w["wA_out"] = f32(inp["L0_attn_a_W_out"])
    w["bA_out"] = np.ascontiguousarray(f32(inp["L0_attn_a_b_out"]).reshape(NCH, 128).T)
    w["rms_c"] = np.ascontiguousarray(f32(inp["L0_attn_b_rms_scale"]).reshape(NCH, 128).T)
    w["split_c"] = np.ascontiguousarray(f32(inp["L0_attn_b_split_scale"]).reshape(NCH, 128).T)
    w["wC_K"] = f32(inp["L0_attn_b_W_K"])
    w["wC_ugg"] = f32(inp["L0_attn_b_W_ugg"])
    w["wC_out"] = f32(inp["L0_attn_b_W_out"])
    w["wF0"] = np.ascontiguousarray(f32(inp["L0_ffn_weight"]).T)
    w["wE_q"] = f32(inp["L1_attn_a_W_Q"])
    w["wE_k"] = f32(inp["L1_attn_a_W_K"])
    w["wE_v"] = f32(inp["L1_attn_a_W_V"])
    w["wG_q"] = np.ascontiguousarray(f32(inp["L1_attn_b_Wq"]).T)
    w["wG_k"] = np.ascontiguousarray(f32(inp["L1_attn_b_Wk"]).T)
    w["wG_v"] = np.ascontiguousarray(f32(inp["L1_attn_b_Wv"]).T)
    w["wG_o"] = np.ascontiguousarray(f32(inp["L1_attn_b_Wo"]).T)
    w["gateT"] = np.ascontiguousarray(f32(inp["L1_ffn_gate_w"]).T)
    w["ebias"] = np.ascontiguousarray(np.tile(f32(inp["L1_ffn_expert_bias"])[None, :], (128, 1)))
    w["wg_t"] = np.ascontiguousarray(f32(inp["L1_ffn_w_gate"]).transpose(0, 2, 1))
    w["wu_t"] = np.ascontiguousarray(f32(inp["L1_ffn_w_up"]).transpose(0, 2, 1))
    w["wd_t"] = np.ascontiguousarray(f32(inp["L1_ffn_w_down"]).transpose(0, 2, 1))
    w["sg_t"] = np.ascontiguousarray(f32(inp["L1_ffn_shared_gate_w"]).T)
    w["su_t"] = np.ascontiguousarray(f32(inp["L1_ffn_shared_up_w"]).T)
    w["sd_t"] = np.ascontiguousarray(f32(inp["L1_ffn_shared_down_w"]).T)
    return w


WEIGHT_SHAPES = {
    "wA_qkv": [D, 3072], "bA_qkv": [128, 3072], "wA_out": [D, D], "bA_out": [128, NCH],
    "rms_c": [128, NCH], "split_c": [128, NCH], "wC_K": [D, 2048], "wC_ugg": [D, 6144],
    "wC_out": [2048, D], "wF0": [D, D], "wE_q": [D, D], "wE_k": [D, D], "wE_v": [D, D],
    "wG_q": [D, D], "wG_k": [D, D], "wG_v": [D, D], "wG_o": [D, D],
    "gateT": [D, NEXP], "ebias": [128, NEXP],
    "wg_t": [NEXP, D, D], "wu_t": [NEXP, D, D], "wd_t": [NEXP, D, D],
    "sg_t": [D, D], "su_t": [D, D], "sd_t": [D, D],
    "gloc": [128, 128], "gloc_tri": [128, 128], "cbank": [128, C_NCONST],
    "ltmask": [128, NEXP * NEXP],
}
TABLE_SHAPES_BF = {
    "rotm": [128, 128], "ones_bf": [128, 128], "ident_bf": [128, 128],
    "onehots": [128, NEXP * 128],
}



def ln_scalar_casts(s, bufs, xr_ap, c, pcs):
    xbf, sqbf = bufs["xbf"], bufs["sqbf"]
    NT = xbf[0].shape[1]
    src = xr_ap[:, c * NT:(c + 1) * NT]
    pcs.inc(s.activation(xbf[c % 2][:], src, AF.Copy), 1)
    pcs.inc(s.activation(sqbf[c % 2][:], src, AF.Square), 1)


def ln_finalize_vector(v, bufs, cb):
    ps_s, ps_q = bufs["ps_s"], bufs["ps_q"]
    mneg, rstd, tmp1 = bufs["mneg"], bufs["rstd"], bufs["tmp1"]
    v.tensor_scalar(mneg[:], ps_s[:], -1.0 / D, None, ALU.mult)
    v.tensor_tensor(tmp1[:], mneg[:], mneg[:], ALU.mult)
    inst = v.scalar_tensor_tensor(rstd[:], ps_q[:], 1.0 / D, tmp1[:], ALU.mult, ALU.subtract)
    return inst



def emit_phase_ffn(C, src, dst):
    nc, block, T = C.nc, C.block, C.T
    NT = min(512, T)
    nchunk = T // NT
    st = C.st
    st0 = st.n
    pxl, pcs, psm, pvp, psq, pxh, ppm, pev, gw = C.phase_sems(
        ["xl", "cs", "sm", "vp", "sq", "xh", "pm", "ev", "gw"])

    es2 = contextlib.ExitStack()
    sb = lambda n, sh, dt: es2.enter_context(nc.sbuf_tensor(f"f{C.pid}_{n}", sh, dt))
    xr = [sb(f"xr{i}", [128, NCH * NT], F32) for i in range(2)]
    xbf = [sb(f"xbf{i}", [128, NT], BF16) for i in range(2)]
    sqbf = [sb(f"sqbf{i}", [128, NT], BF16) for i in range(2)]
    xh = sb("xh", [128, NCH * NT], BF16)
    mneg = sb("mneg", [128, NT], F32)
    rstd = sb("rstd", [128, NT], F32)
    tmp1 = sb("tmp1", [128, NT], F32)
    tmp2 = sb("tmp2", [128, NT], F32)
    ps_s = es2.enter_context(nc.psum_tensor(f"f{C.pid}_ps_s", [128, NT], F32))
    ps_q = es2.enter_context(nc.psum_tensor(f"f{C.pid}_ps_q", [128, NT], F32))
    ps_p = [es2.enter_context(nc.psum_tensor(f"f{C.pid}_ps_p{i}", [128, NT], F32)) for i in range(2)]
    bufs = {"xbf": xbf, "sqbf": sqbf, "ps_s": ps_s, "ps_q": ps_q,
            "mneg": mneg, "rstd": rstd, "tmp1": tmp1}

    weff = sb0_weff = es2.enter_context(nc.sbuf_tensor(f"f{C.pid}_weff", [128, NCH * D], BF16))
    prep = True
    if prep:
        wfl = sb("wfl", [128, D], F32)
        wt1 = sb("wt1", [128, D], F32)
        wt2 = sb("wt2", [128, D], F32)
        wabs = sb("wabs", [128, NCH], F32)
        wabsb = sb("wabsb", [128, NCH], BF16)
        wsc = sb("wsc", [128, 4], F32)
        ps_w = es2.enter_context(nc.psum_tensor(f"f{C.pid}_ps_w", [128, NCH], F32))

        @block.sync
        def _(sync):
            sync.wait_ge(st.h, st0)
            for c in range(NCH):
                if c > 0:
                    gw.wait(sync, c)
                pxl.inc(sync.dma_start(out=wfl[:], in_=C.dram["wF0"][128 * c:128 * c + 128, :]), 16)

        @block.vector
        def _(v):
            for c in range(NCH):
                pxl.wait(v, 16 * (c + 1))
                gw.inc(v.tensor_reduce(wabs[:, c:c + 1], wfl[:], axis=AX.X, op=ALU.add,
                                       apply_absolute_value=True), 1)

        @block.scalar
        def _(s):
            gw.wait(s, NCH)
            gw.inc(s.activation(wabsb[:], wabs[:], AF.Copy), 1)

        @block.tensor
        def _(t):
            t.wait_ge(C.sem("ldglob").h, 48)
            gw.wait(t, NCH + 1)
            gw.inc(t.matmul(ps_w[:], C.glob["ones"][:], wabsb[:], start=True, stop=True), 1)

        @block.vector
        def _(v):
            gw.wait(v, NCH + 2)
            C.vdrain(v, v.tensor_reduce(wsc[:, 3:4], ps_w[:], axis=AX.X, op=ALU.add))
            C.vdrain(v, v.tensor_scalar(wsc[:, 2:3], wsc[:, 3:4], 1.0 / (D * D), None, ALU.mult))
            C.vdrain(v, v.tensor_scalar(wsc[:, 2:3], wsc[:, 2:3], 1e-5, None, ALU.max))
            v.tensor_scalar(wsc[:, 0:1], wsc[:, 2:3], 0.5, None, ALU.mult)
            C.vdrain(v, v.tensor_scalar(wsc[:, 1:2], wsc[:, 2:3], -0.5, None, ALU.mult))
            gw.inc(v.tensor_copy(wsc[:, 3:4], wsc[:, 3:4]), 1)

        @block.sync
        def _(sync):
            for c in range(NCH):
                gw.wait(sync, NCH + 3 + c)
                pxl.inc(sync.dma_start(out=wfl[:], in_=C.dram["wF0"][128 * c:128 * c + 128, :]), 16)

        @block.vector
        def _(v):
            for c in range(NCH):
                pxl.wait(v, 16 * (NCH + c + 1))
                v.tensor_scalar(wt1[:], wfl[:], wsc[:, 0:1], None, ALU.is_gt)
                v.tensor_scalar(wt2[:], wfl[:], wsc[:, 1:2], None, ALU.is_lt)
                v.tensor_tensor(wt1[:], wt1[:], wt2[:], ALU.subtract)
                gw.inc(v.tensor_scalar(weff[:, D * c:D * (c + 1)], wt1[:], wsc[:, 2:3],
                                       None, ALU.mult), 1)
    wprep_done = gw.n - gw.b

    if prep and getattr(C, "dbg_b", False):
        C.dbg_b = False

        @block.sync
        def _(sync):
            gw.wait(sync, wprep_done)
            st.inc(sync.dma_start(out=C.dram["dbg_weff"], in_=weff[:]), 16)
            st.inc(sync.dma_start(out=C.dram["dbg_wsc"], in_=wsc[:]), 16)
            st.inc(sync.dma_start(out=C.dram["dbg_wabs"], in_=wabs[:]), 16)

    @block.sync
    def _(sync):
        sync.wait_ge(st.h, st0)
        for i in range(nchunk):
            if i >= 2:
                sync.wait_ge(st.h, st0 + 16 * NCH * (i - 1))
            for c in range(NCH):
                pxl.inc(sync.dma_start(out=xr[i % 2][:, c * NT:(c + 1) * NT],
                                       in_=src[128 * c:128 * c + 128, i * NT:(i + 1) * NT]), 16)
            if i >= 1:
                pev.wait(sync, NCH * i)
                for c in range(NCH):
                    st.inc(sync.dma_start(out=dst[128 * c:128 * c + 128, (i - 1) * NT:i * NT],
                                          in_=xr[(i - 1) % 2][:, c * NT:(c + 1) * NT]), 16)
        pev.wait(sync, NCH * nchunk)
        i = nchunk - 1
        for c in range(NCH):
            st.inc(sync.dma_start(out=dst[128 * c:128 * c + 128, i * NT:(i + 1) * NT],
                                  in_=xr[i % 2][:, c * NT:(c + 1) * NT]), 16)

    pxl_base = 16 * (2 * NCH if prep else 0)

    @block.scalar
    def _(s):
        for i in range(nchunk):
            for c in range(NCH):
                pxl.wait(s, pxl_base + 16 * (NCH * i + c + 1))
                if NCH * i + c >= 2:
                    psm.wait(s, 2 * (NCH * i + c - 1))
                ln_scalar_casts(s, bufs, xr[i % 2], c, pcs)
            pvp.wait(s, i + 1)
            psq.inc(s.activation(tmp2[:], rstd[:], AF.Sqrt, bias=C.glob["cb"][:, C_LNEPS:C_LNEPS + 1]), 1)

    @block.tensor
    def _(t):
        t.wait_ge(C.sem("ldglob").h, 48)
        if prep:
            gw.wait(t, wprep_done)
        for i in range(nchunk):
            if i >= 1:
                pvp.wait(t, i)
            for c in range(NCH):
                pcs.wait(t, 2 * (NCH * i + c + 1))
                t.matmul(ps_s[:], C.glob["ones"][:], xbf[c % 2][:], start=(c == 0), stop=(c == NCH - 1))
                inst = t.matmul(ps_q[:], C.glob["ones"][:], sqbf[c % 2][:], start=(c == 0), stop=(c == NCH - 1))
                psm.inc(inst, 2)
            pxh.wait(t, i + 1)
            for o in range(NCH):
                idx = NCH * i + o
                if idx >= 2:
                    pev.wait(t, idx - 1)
                for k in range(NCH):
                    inst = t.matmul(ps_p[o % 2][:],
                                    weff[:, D * k + 128 * o:D * k + 128 * (o + 1)],
                                    xh[:, k * NT:(k + 1) * NT], start=(k == 0), stop=(k == NCH - 1))
                ppm.inc(inst, 1)

    @block.vector
    def _(v):
        for i in range(nchunk):
            psm.wait(v, 2 * NCH * (i + 1))
            pvp.inc(ln_finalize_vector(v, bufs, C.glob["cb"]), 1)
            psq.wait(v, i + 1)
            v.reciprocal(rstd[:], tmp2[:])
            for c in range(NCH):
                v.tensor_tensor(tmp1[:], xr[i % 2][:, c * NT:(c + 1) * NT], mneg[:], ALU.add)
                inst = v.tensor_tensor(xh[:, c * NT:(c + 1) * NT], tmp1[:], rstd[:], ALU.mult)
            pxh.inc(inst, 1)
            for o in range(NCH):
                ppm.wait(v, NCH * i + o + 1)
                sl = xr[i % 2][:, o * NT:(o + 1) * NT]
                pev.inc(v.tensor_tensor(sl, sl, ps_p[o % 2][:], ALU.add), 1)

    es2.close()
    C.barrier_end()



def kernel(**inputs):
    return _run(inputs, ncores=8)


def _run(inputs, T=8192, S=4096, phases="ABCDEFGH", ncores=8, sim=False, dbg=False):
    w = prep_weights(inputs)
    tabs = make_tables(S)
    x = np.asarray(inputs["L0_attn_a_x"], dtype=np.float32)
    B = x.shape[0]
    nseq = T // S
    nc_, C = build_program(T=T, S=S, phases=phases, dbg=dbg)
    in_maps = []
    for core in range(ncores):
        m = dict(w)
        m.update(tabs)
        xs = x[core * nseq:(core + 1) * nseq, :S, :].reshape(nseq * S, D)
        m["x"] = np.ascontiguousarray(xs.T)
        in_maps.append(m)
    if sim:
        from concourse.bass_interp import MultiCoreSim
        simu = MultiCoreSim(nc_, num_cores=ncores, num_workers=1, trace=False)
        for i in range(ncores):
            for k, v in in_maps[i].items():
                simu.cores[i].tensor(k)[:] = v
        simu.simulate(check_with_hw=False)
        outs = [np.asarray(simu.cores[i].tensor("out")) for i in range(ncores)]
        if dbg:
            _run.dbg_data = {k: np.asarray(simu.cores[0].tensor(k))
                             for k in ["dbg_scb", "dbg_cmbP", "dbg_xh", "dbg_h", "dbg_h0", "dbg_cm0"]}
    else:
        res = run_bass_kernel_spmd(nc_, in_maps, core_ids=list(range(ncores)))
        outs = [np.asarray(r["out"]) for r in res.results]
    full = np.concatenate([o.T.reshape(nseq, S, D) for o in outs], axis=0)
    return full



def emit_phase_attn_a(C, src, dst):
    nc, block, T = C.nc, C.block, C.T
    NT = min(512, T)
    nchunk = T // NT
    TPC = NT // 128
    st = C.st
    st0 = st.n
    (pxl, pcs, psm, pvp, psq, pxh, pqm, pqe, pms, pmv, pmg, pob, ptr, pte,
     pwm, pox, pwl, pbq) = C.phase_sems(
        ["a", "b", "c", "d", "e", "f", "g", "h", "i", "j", "k", "l", "m",
         "n", "o", "p", "q", "r"])
    pwl = C.fresh()

    es2 = contextlib.ExitStack()
    sb = lambda n, sh, dt: es2.enter_context(nc.sbuf_tensor(f"a{C.pid}_{n}", sh, dt))
    ps = lambda n, sh, dt: es2.enter_context(nc.psum_tensor(f"a{C.pid}_{n}", sh, dt))
    xr = [sb(f"xr{j}", [128, NCH * NT], F32) for j in range(2)]
    xh = sb("xh", [128, NCH * NT], BF16)
    xbf = [sb(f"xbf{j}", [128, NT], BF16) for j in range(2)]
    sqbf = [sb(f"sqbf{j}", [128, NT], BF16) for j in range(2)]
    mneg = sb("mneg", [128, NT], F32)
    rstd = sb("rstd", [128, NT], F32)
    tmp1 = sb("tmp1", [128, NT], F32)
    tmp2 = sb("tmp2", [128, NT], F32)
    wqkvb = sb("wqkvb", [128, NCH * 3072], BF16)
    woutb = sb("woutb", [128, NCH * D], BF16)
    bqkv = sb("bqkv", [128, 3072], F32)
    qt = sb("qt", [128, D], F32)
    kt = sb("kt", [128, D], F32)
    vt = sb("vt", [128, D], F32)
    qsq = sb("qsq", [128, 16], F32)
    ksq = sb("ksq", [128, 16], F32)
    mq = sb("mq", [128, 16], F32)
    mk = sb("mk", [128, 16], F32)
    qkm = sb("qkm", [128, 256], F32)
    prodV = sb("prodV", [128, D], F32)
    prodG = sb("prodG", [128, D], F32)
    dm3 = sb("dm3", [128, 256], F32)
    su = sb("su", [128, 256], F32)
    w1 = sb("w1", [128, 256], F32)
    w2 = sb("w2", [128, 256], F32)
    cosv = sb("cosv", [128, 256], F32)
    att = sb("att", [128, 256], F32)
    mx16 = sb("mx16", [128, 16], F32)
    outV = sb("outV", [128, D], F32)
    outG = sb("outG", [128, D], F32)
    outbf = sb("outbf", [128, D], BF16)
    ofmb = sb("ofmb", [128, D], BF16)
    psS = ps("psS", [128, NT], F32)
    psQt = ps("psQt", [128, NT], F32)
    psH = ps("psH", [128, 1536], F32)
    psTr = ps("psTr", [128, D], BF16)
    psOut = ps("psOut", [128, D], F32)
    eps6 = C.glob["cb"][:, C_EPS6:C_EPS6 + 1]
    hview = lambda b: b[:].rearrange("p (h g) -> p h g", h=16)

    @block.gpsimd
    def _(g):
        g.wait_ge(st.h, st0)
        for k in range(NCH):
            pwl.inc(g.dma_start(out=wqkvb[:, 3072 * k:3072 * (k + 1)],
                                in_=C.dram["wA_qkv"][128 * k:128 * k + 128, :]), 16)
        for k in range(NCH):
            pwl.inc(g.dma_start(out=woutb[:, D * k:D * (k + 1)],
                                in_=C.dram["wA_out"][128 * k:128 * k + 128, :]), 16)
        for i in range(nchunk):
            for tt in range(TPC):
                gti = i * TPC + tt
                pmv.wait(g, 5 * gti + 5)
                for gg in range(16):
                    asl = hview(att)[:, :, gg].unsqueeze(2).broadcast_to([128, 16, 64])
                    vg = vt[:, 64 * gg:64 * (gg + 1)].unsqueeze(1).broadcast_to([128, 16, 64])
                    if gg == 0:
                        inst = g.tensor_tensor(outG[:].rearrange("p (h g) -> p h g", h=16), asl, vg, ALU.mult)
                    else:
                        g.tensor_tensor(hview(prodG), asl, vg, ALU.mult)
                        inst = g.tensor_tensor(outG[:], outG[:], prodG[:], ALU.add)
                pmg.inc(inst, 1)

    @block.sync
    def _(sync):
        sync.wait_ge(st.h, st0)
        pbq.inc(sync.dma_start(out=bqkv[:], in_=C.dram["bA_qkv"]), 16)
        for i in range(nchunk):
            if i >= 2:
                sync.wait_ge(st.h, st0 + 16 * NCH * (i - 1))
            for c in range(NCH):
                pxl.inc(sync.dma_start(out=xr[i % 2][:, c * NT:(c + 1) * NT],
                                       in_=src[128 * c:128 * c + 128, i * NT:(i + 1) * NT]), 16)
            pox.wait(sync, NCH * TPC * (i + 1))
            for c in range(NCH):
                st.inc(sync.dma_start(out=dst[128 * c:128 * c + 128, i * NT:(i + 1) * NT],
                                      in_=xr[i % 2][:, c * NT:(c + 1) * NT]), 16)

    @block.scalar
    def _(s):
        for i in range(nchunk):
            pxl.wait(s, 16 * NCH * (i + 1))
            for c in range(NCH):
                idx = i * NCH + c
                if idx >= 2:
                    psm.wait(s, 2 * (idx - 1))
                srcap = xr[i % 2][:, c * NT:(c + 1) * NT]
                pcs.inc(s.activation(xbf[idx % 2][:], srcap, AF.Copy), 1)
                pcs.inc(s.activation(sqbf[idx % 2][:], srcap, AF.Square), 1)
            pvp.wait(s, i + 1)
            psq.inc(s.activation(tmp2[:], rstd[:], AF.Sqrt,
                                 bias=C.glob["cb"][:, C_LNEPS:C_LNEPS + 1]), 1)
            for tt in range(TPC):
                gti = i * TPC + tt
                pmv.wait(s, 5 * gti + 1)
                s.activation(mq[:], qsq[:], AF.Sqrt)
                pms.inc(s.activation(mk[:], ksq[:], AF.Sqrt), 1)
                pmv.wait(s, 5 * gti + 2)
                pms.inc(s.activation(su[:], dm3[:], AF.Sqrt, bias=eps6), 1)
                pmv.wait(s, 5 * gti + 3)
                pms.inc(s.activation(cosv[:], w1[:], AF.Sin), 1)
                pmv.wait(s, 5 * gti + 4)
                pms.inc(s.activation(att[:], w2[:], AF.Exp), 1)

    @block.tensor
    def _(t):
        t.wait_ge(C.sem("ldglob").h, 48)
        pwl.wait(t, 16 * 2 * NCH)
        for i in range(nchunk):
            if i >= 1:
                pvp.wait(t, i)
            for c in range(NCH):
                idx = i * NCH + c
                pcs.wait(t, 2 * (idx + 1))
                t.matmul(psS[:], C.glob["ones"][:], xbf[idx % 2][:], start=(c == 0), stop=(c == NCH - 1))
                inst = t.matmul(psQt[:], C.glob["ones"][:], sqbf[idx % 2][:], start=(c == 0), stop=(c == NCH - 1))
                psm.inc(inst, 2)
            pxh.wait(t, i + 1)
            for tt in range(TPC):
                gti = i * TPC + tt
                for half in range(2):
                    if gti > 0 or half > 0:
                        pqe.wait(t, 2 * gti + half)
                    for k in range(NCH):
                        for n3 in range(3):
                            inst = t.matmul(psH[:, 512 * n3:512 * (n3 + 1)],
                                            xh[:, k * NT + tt * 128: k * NT + (tt + 1) * 128],
                                            wqkvb[:, 3072 * k + 1536 * half + 512 * n3:
                                                  3072 * k + 1536 * half + 512 * (n3 + 1)],
                                            start=(k == 0), stop=(k == NCH - 1))
                    pqm.inc(inst, 1)
                pob.wait(t, gti + 1)
                if gti >= 1:
                    pte.wait(t, gti)
                for c in range(NCH):
                    inst = t.transpose(psTr[:, 128 * c:128 * (c + 1)], outbf[:, 128 * c:128 * (c + 1)],
                                       C.glob["ident"][:])
                ptr.inc(inst, 1)
                pte.wait(t, gti + 1)
                if gti >= 1:
                    pox.wait(t, NCH * gti)
                for o in range(NCH):
                    for k in range(NCH):
                        inst = t.matmul(psOut[:, 128 * o:128 * (o + 1)],
                                        woutb[:, D * k + 128 * o: D * k + 128 * (o + 1)],
                                        ofmb[:, 128 * k:128 * (k + 1)], start=(k == 0), stop=(k == NCH - 1))
                pwm.inc(inst, 1)

    @block.vector
    def _(v):
        v.wait_ge(C.sem("ldglob").h, 96)
        pbq.wait(v, 16)
        for i in range(nchunk):
            psm.wait(v, 2 * NCH * (i + 1))
            v.tensor_scalar(mneg[:], psS[:], -1.0 / D, None, ALU.mult)
            v.tensor_tensor(tmp1[:], mneg[:], mneg[:], ALU.mult)
            pvp.inc(v.scalar_tensor_tensor(rstd[:], psQt[:], 1.0 / D, tmp1[:], ALU.mult, ALU.subtract), 1)
            psq.wait(v, i + 1)
            v.reciprocal(rstd[:], tmp2[:])
            for c in range(NCH):
                xsl_ = xr[i % 2][:, c * NT:(c + 1) * NT]
                v.tensor_tensor(tmp1[:], xsl_, mneg[:], ALU.add)
                inst = v.tensor_tensor(xh[:, c * NT:(c + 1) * NT], tmp1[:], rstd[:], ALU.mult)
            pxh.inc(inst, 1)
            for tt in range(TPC):
                gti = i * TPC + tt
                for half in range(2):
                    pqm.wait(v, 2 * gti + half + 1)
                    for n3 in range(3):
                        col = 1536 * half + 512 * n3
                        dstb = [qt, kt, vt][col // 1024]
                        inst = v.tensor_tensor(dstb[:, col % 1024: col % 1024 + 512],
                                               psH[:, 512 * n3:512 * (n3 + 1)],
                                               bqkv[:, col:col + 512], ALU.add)
                    pqe.inc(inst, 1)
                v.tensor_tensor(prodV[:], qt[:], qt[:], ALU.mult)
                v.tensor_reduce(qsq[:], hview(prodV), axis=AX.X, op=ALU.add)
                v.tensor_tensor(prodV[:], kt[:], kt[:], ALU.mult)
                pmv.inc(v.tensor_reduce(ksq[:], hview(prodV), axis=AX.X, op=ALU.add), 1)
                for h in range(16):
                    v.tensor_tensor(hview(prodV), hview(kt),
                                    qt[:, 64 * h:64 * (h + 1)].unsqueeze(1).broadcast_to([128, 16, 64]),
                                    ALU.mult)
                    v.tensor_reduce(qkm[:, 16 * h:16 * (h + 1)], hview(prodV), axis=AX.X, op=ALU.add)
                v.tensor_tensor(hview(dm3), qsq[:].unsqueeze(2).broadcast_to([128, 16, 16]),
                                ksq[:].unsqueeze(1).broadcast_to([128, 16, 16]), ALU.add)
                v.scalar_tensor_tensor(w1[:], qkm[:], -2.0, dm3[:], ALU.mult, ALU.add)
                pmv.inc(v.tensor_scalar(dm3[:], w1[:], 0.0, None, ALU.max), 1)
                pms.wait(v, 4 * gti + 2)
                v.tensor_scalar(w1[:], su[:], PI / 2, None, ALU.add)
                v.tensor_scalar(w2[:], w1[:], PI, None, ALU.is_gt)
                v.scalar_tensor_tensor(su[:], w2[:], -2 * PI, w1[:], ALU.mult, ALU.add)
                v.tensor_scalar(w2[:], su[:], PI, None, ALU.is_gt)
                v.scalar_tensor_tensor(w1[:], w2[:], -2 * PI, su[:], ALU.mult, ALU.add)
                v.tensor_scalar(w2[:], w1[:], PI, None, ALU.is_gt)
                v.scalar_tensor_tensor(su[:], w2[:], -2 * PI, w1[:], ALU.mult, ALU.add)
                v.tensor_scalar(w2[:], su[:], PI, None, ALU.is_gt)
                inst = v.scalar_tensor_tensor(w1[:], w2[:], -2 * PI, su[:], ALU.mult, ALU.add)
                pmv.inc(inst, 1)
                pms.wait(v, 4 * gti + 3)
                v.tensor_scalar(w2[:], cosv[:], 0.15, 1.0, ALU.mult, ALU.add)
                v.tensor_tensor(w2[:], w2[:], dm3[:], ALU.mult)
                v.tensor_scalar(w2[:], w2[:], 1e-6, None, ALU.add)
                v.reciprocal(w2[:], w2[:])
                v.tensor_tensor(hview(w1), mq[:].unsqueeze(2).broadcast_to([128, 16, 16]),
                                mk[:].unsqueeze(1).broadcast_to([128, 16, 16]), ALU.mult)
                v.tensor_tensor(w1[:], w1[:], w2[:], ALU.mult)
                C.vdrain(v, v.reduce_max(mx16[:], hview(w1), axis=AX.X))
                pmv.inc(v.tensor_tensor(hview(w2), hview(w1),
                                        mx16[:].unsqueeze(2).broadcast_to([128, 16, 16]), ALU.subtract), 1)
                pms.wait(v, 4 * (gti + 1))
                C.vdrain(v, v.reduce_sum(mx16[:], hview(att), axis=AX.X))
                C.vdrain(v, v.reciprocal(mx16[:], mx16[:]))
                pmv.inc(v.tensor_tensor(hview(att), hview(att),
                                        mx16[:].unsqueeze(2).broadcast_to([128, 16, 16]), ALU.mult), 1)
                pmg.wait(v, gti + 1)
                pob.inc(v.tensor_copy(outbf[:], outG[:]), 1)
                ptr.wait(v, gti + 1)
                pte.inc(v.tensor_copy(ofmb[:], psTr[:]), 1)
                pwm.wait(v, gti + 1)
                for o in range(NCH):
                    xsl_ = xr[i % 2][:, o * NT + tt * 128: o * NT + (tt + 1) * 128]
                    v.tensor_tensor(tmp1[:, 0:128], psOut[:, 128 * o:128 * (o + 1)], xsl_, ALU.add)
                    inst = v.tensor_scalar(xsl_, tmp1[:, 0:128], C.glob["bAout_sb"][:, o:o + 1], None, ALU.add)
                    pox.inc(inst, 1)

    es2.close()


def emit_phase_attn_b(C, src, dst):
    nc, block, T, S = C.nc, C.block, C.T, C.S
    TB = min(1024, S)
    NT = min(512, TB)
    SUB = TB // NT
    nblk = S // TB
    NSEQ = T // S
    NCC = 16
    st = C.st
    st0 = st.n
    (pxl, pcs, psm, pvp, psq, pxn, pwl, pwo, pkm, pks, psg, pbb, psc, pyb,
     pom, pox, pgl2) = C.phase_sems(
        ["xl", "cs", "sm", "vp", "sq", "xn", "wl", "wo", "km", "ks", "sg",
         "bb", "sc", "yb", "om", "ox", "gl2"])
    pwl, pwo = C.fresh(), C.fresh()

    es2 = contextlib.ExitStack()
    sb = lambda n, sh, dt: es2.enter_context(nc.sbuf_tensor(f"c{C.pid}_{n}", sh, dt))
    ps = lambda n, sh: es2.enter_context(nc.psum_tensor(f"c{C.pid}_{n}", sh, F32))
    xr = sb("xr", [128, NCH * TB], F32)
    xn = sb("xn", [128, NCH * TB], BF16)
    mneg = sb("mneg", [128, TB], F32)
    rstd = sb("rstd", [128, TB], F32)
    Scv = sb("Scv", [128, TB], F32)
    tmp1 = sb("tmp1", [128, NT], F32)
    tmp2 = sb("tmp2", [128, NT], F32)
    tmp3 = sb("tmp3", [128, NT], F32)
    xbf = [sb(f"xbf{i}", [128, NT], BF16) for i in range(2)]
    sqbf = [sb(f"sqbf{i}", [128, NT], BF16) for i in range(2)]
    rsss = sb("rsss", [128, NCH], F32)
    ksig = sb("ksig", [128, TB], F32)
    abuf = sb("abuf", [128, TB], F32)
    bbuf = sb("bbuf", [128, TB], F32)
    gobuf = sb("gobuf", [128, TB], F32)
    Hbuf = sb("Hbuf", [128, TB], F32)
    ybuf = sb("ybuf", [128, NCC * TB], BF16)
    kcar = sb("kcar", [128, NCC], F32)
    hcar = sb("hcar", [128, NCC], F32)
    wslab = [sb(f"wsl{j}", [128, 4 * NCH * 128], BF16) for j in range(2)]
    wout = sb("wout", [128, NCC * D], BF16)
    ps0 = ps("ps0", [128, NT])
    ps1 = ps("ps1", [128, NT])
    ps2 = ps("ps2", [128, NT])
    ps3 = ps("ps3", [128, NT])
    psS = ps("psS", [128, NT])
    psQ = ps("psQ", [128, NT])
    psO = [ps(f"psO{j}", [128, NT]) for j in range(2)]

    @block.gpsimd
    def _(g):
        g.wait_ge(st.h, st0)
        for cc in range(NCC):
            pwo.inc(g.dma_start(out=wout[:, D * cc:D * (cc + 1)],
                                in_=C.dram["wC_out"][128 * cc:128 * cc + 128, :]), 16)
        for blk in range(NSEQ * nblk):
            for cc in range(NCC):
                gb = blk * NCC + cc
                if gb >= 2:
                    pkm.wait(g, SUB * (gb - 1))
                w = wslab[gb % 2]
                for k in range(NCH):
                    pwl.inc(g.dma_start(
                        out=w[:, (0 * NCH + k) * 128:(0 * NCH + k + 1) * 128],
                        in_=C.dram["wC_K"][128 * k:128 * k + 128, 128 * cc:128 * (cc + 1)]), 16)
                for j in range(3):
                    for k in range(NCH):
                        pwl.inc(g.dma_start(
                            out=w[:, ((j + 1) * NCH + k) * 128:((j + 1) * NCH + k + 1) * 128],
                            in_=C.dram["wC_ugg"][128 * k:128 * k + 128,
                                                 2048 * j + 128 * cc:2048 * j + 128 * (cc + 1)]), 16)

    @block.sync
    def _(sync):
        sync.wait_ge(st.h, st0)
        for blk in range(NSEQ * nblk):
            t0 = blk * TB
            if blk >= 1:
                sync.wait_ge(st.h, st0 + 16 * NCH * blk)
            for c in range(NCH):
                pxl.inc(sync.dma_start(out=xr[:, c * TB:(c + 1) * TB],
                                       in_=src[128 * c:128 * c + 128, t0:t0 + TB]), 16)
            pox.wait(sync, NCH * SUB * (blk + 1))
            for c in range(NCH):
                st.inc(sync.dma_start(out=dst[128 * c:128 * c + 128, t0:t0 + TB],
                                      in_=xr[:, c * TB:(c + 1) * TB]), 16)

    @block.scalar
    def _(s):
        for blk in range(NSEQ * nblk):
            for sub in range(SUB):
                sg = blk * SUB + sub
                pxl.wait(s, 16 * NCH * (blk + 1))
                for c in range(NCH):
                    idx = sg * NCH + c
                    if idx >= 2:
                        psm.wait(s, 2 * (idx - 1))
                    srcap = xr[:, c * TB + sub * NT: c * TB + (sub + 1) * NT]
                    pcs.inc(s.activation(xbf[idx % 2][:], srcap, AF.Copy), 1)
                    pcs.inc(s.activation(sqbf[idx % 2][:], srcap, AF.Square), 1)
                pvp.wait(s, 2 * sg + 1)
                psq.inc(s.activation(tmp2[:], rstd[:, sub * NT:(sub + 1) * NT], AF.Sqrt,
                                     bias=C.glob["cb"][:, C_LNEPS:C_LNEPS + 1]), 1)
                pvp.wait(s, 2 * sg + 2)
                psq.inc(s.activation(tmp2[:], tmp3[:], AF.Sqrt), 1)
            for cc in range(NCC):
                gb = blk * NCC + cc
                if gb >= 1:
                    pyb.wait(s, gb)
                for sub in range(SUB):
                    pkm.wait(s, gb * SUB + sub + 1)
                    if gb * SUB + sub >= 1:
                        pbb.wait(s, 2 * (gb * SUB + sub))
                    sl = slice(sub * NT, (sub + 1) * NT)
                    psg.inc(s.activation(ksig[:, sl], ps0[:], AF.Sigmoid), 1)
                    psg.inc(s.activation(tmp1[:], ps2[:], AF.Sigmoid), 1)
                    psg.inc(s.activation(gobuf[:, sl], ps3[:], AF.Sigmoid), 1)

    @block.tensor
    def _(t):
        t.wait_ge(C.sem("ldglob").h, 48)
        pwo.wait(t, 16 * NCC)
        for blk in range(NSEQ * nblk):
            for sub in range(SUB):
                sg = blk * SUB + sub
                if sg >= 1:
                    pvp.wait(t, 2 * sg)
                for c in range(NCH):
                    idx = sg * NCH + c
                    pcs.wait(t, 2 * (idx + 1))
                    t.matmul(psS[:], C.glob["ones"][:], xbf[idx % 2][:], start=(c == 0), stop=(c == NCH - 1))
                    inst = t.matmul(psQ[:], C.glob["ones"][:], sqbf[idx % 2][:], start=(c == 0), stop=(c == NCH - 1))
                    psm.inc(inst, 2)
            for cc in range(NCC):
                gb = blk * NCC + cc
                pwl.wait(t, 16 * 4 * NCH * (gb + 1))
                pxn.wait(t, SUB * (blk + 1))
                w = wslab[gb % 2]
                for sub in range(SUB):
                    if gb * SUB + sub >= 1:
                        pbb.wait(t, 2 * (gb * SUB + sub))
                    xsl = lambda k, sub=sub: xn[:, k * TB + sub * NT: k * TB + (sub + 1) * NT]
                    for j, pdst in enumerate([ps0, ps1, ps2, ps3]):
                        for k in range(NCH):
                            inst = t.matmul(pdst[:], w[:, (j * NCH + k) * 128:(j * NCH + k + 1) * 128],
                                            xsl(k), start=(k == 0), stop=(k == NCH - 1))
                    pkm.inc(inst, 1)
            pyb.wait(t, NCC * (blk + 1))
            for o in range(NCH):
                for sub in range(SUB):
                    go = (blk * NCH + o) * SUB + sub
                    pO = psO[(o * SUB + sub) % 2]
                    if go >= 2:
                        pox.wait(t, go - 1)
                    for cc in range(NCC):
                        inst = t.matmul(pO[:], wout[:, D * cc + 128 * o: D * cc + 128 * (o + 1)],
                                        ybuf[:, cc * TB + sub * NT: cc * TB + (sub + 1) * NT],
                                        start=(cc == 0), stop=(cc == NCC - 1))
                    pom.inc(inst, 1)

    @block.vector
    def _(v):
        v.wait_ge(C.sem("ldglob").h, 96)
        v.wait_ge(st.h, st0)
        v.tensor_tensor(rsss[:], C.glob["rms_sb"][:], C.glob["split_sb"][:], ALU.mult)
        for blk in range(NSEQ * nblk):
            seqpos = blk % nblk
            for sub in range(SUB):
                sg = blk * SUB + sub
                sl = slice(sub * NT, (sub + 1) * NT)
                psm.wait(v, 2 * NCH * (sg + 1))
                mn = mneg[:, sl]
                rs = rstd[:, sl]
                v.tensor_scalar(mn, psS[:], -1.0 / D, None, ALU.mult)
                v.tensor_tensor(tmp1[:], mn, mn, ALU.mult)
                pvp.inc(v.scalar_tensor_tensor(rs, psQ[:], 1.0 / D, tmp1[:], ALU.mult, ALU.subtract), 1)
                psq.wait(v, 2 * sg + 1)
                v.tensor_copy(tmp1[:], rs)
                v.reciprocal(rs, tmp2[:])
                v.tensor_tensor(tmp3[:], rs, rs, ALU.mult)
                pvp.inc(v.tensor_tensor(tmp3[:], tmp1[:], tmp3[:], ALU.mult), 1)
                psq.wait(v, 2 * sg + 2)
                v.tensor_scalar(tmp1[:], tmp2[:], 1e-8, None, ALU.add)
                v.reciprocal(tmp3[:], tmp1[:])
                v.tensor_tensor(Scv[:, sl], rs, tmp3[:], ALU.mult)
                for c in range(NCH):
                    xsl_ = xr[:, c * TB + sub * NT: c * TB + (sub + 1) * NT]
                    v.tensor_tensor(tmp1[:], xsl_, mn, ALU.add)
                    v.tensor_tensor(tmp2[:], tmp1[:], Scv[:, sl], ALU.mult)
                    inst = v.tensor_scalar(xn[:, c * TB + sub * NT: c * TB + (sub + 1) * NT],
                                           tmp2[:], rsss[:, c:c + 1], None, ALU.mult)
                pxn.inc(inst, 1)
            for cc in range(NCC):
                gb = blk * NCC + cc
                if cc == 0 and blk >= 1:
                    pom.wait(v, NCH * SUB * blk)
                for sub in range(SUB):
                    sl = slice(sub * NT, (sub + 1) * NT)
                    psg.wait(v, 3 * (gb * SUB + sub + 1))
                    v.tensor_tensor(tmp2[:], ps1[:], tmp1[:], ALU.mult)
                    v.tensor_scalar(tmp3[:], ksig[:, sl], -1.0, 1.0, ALU.mult, ALU.add)
                    pbb.inc(v.tensor_tensor(bbuf[:, sl], tmp2[:], tmp3[:], ALU.mult), 2)
                if getattr(C, "dbg_c", False) and blk == 1 and cc == 0:
                    v.tensor_copy(C.glob["dbg_ks"][:], ksig[:])
                v.tensor_copy(abuf[:, 1:TB], ksig[:, 0:TB - 1])
                if seqpos == 0:
                    v.memset(abuf[:, 0:1], 0.0)
                    init = 0.0
                else:
                    v.tensor_copy(abuf[:, 0:1], kcar[:, cc:cc + 1])
                    init = hcar[:, cc:cc + 1]
                C.vdrain(v, v.tensor_copy(kcar[:, cc:cc + 1], ksig[:, TB - 1:TB]))
                C.vdrain(v, v.tensor_tensor_scan(Hbuf[:], abuf[:], bbuf[:], init, ALU.mult, ALU.add))
                v.tensor_copy(hcar[:, cc:cc + 1], Hbuf[:, TB - 1:TB])
                pyb.inc(v.tensor_tensor(ybuf[:, cc * TB:(cc + 1) * TB], Hbuf[:], gobuf[:], ALU.mult), 1)
            for o in range(NCH):
                for sub in range(SUB):
                    go = (blk * NCH + o) * SUB + sub
                    sl = slice(sub * NT, (sub + 1) * NT)
                    pO = psO[(o * SUB + sub) % 2]
                    pom.wait(v, go + 1)
                    xsl_ = xr[:, o * TB + sub * NT: o * TB + (sub + 1) * NT]
                    v.tensor_tensor(tmp1[:], xsl_, mneg[:, sl], ALU.add)
                    v.tensor_tensor(tmp2[:], tmp1[:], rstd[:, sl], ALU.mult)
                    v.tensor_tensor(tmp1[:], tmp2[:], pO[:], ALU.add)
                    pox.inc(v.tensor_tensor(xsl_, xsl_, tmp1[:], ALU.add), 1)

    if getattr(C, "dbg_c", False):
        dxn = nc.dram_tensor("dbg_xn", [128, NCH * TB], BF16, kind="ExternalOutput").ap()
        dyb = nc.dram_tensor("dbg_yb", [128, NCC * TB], BF16, kind="ExternalOutput").ap()
        dab = nc.dram_tensor("dbg_ab", [128, 4 * TB + 32], F32, kind="ExternalOutput").ap()

        @block.sync
        def _(sync):
            pox.wait(sync, NCH * SUB * NSEQ * nblk)
            st.inc(sync.dma_start(out=dxn, in_=xn[:]), 16)
            st.inc(sync.dma_start(out=dyb, in_=ybuf[:]), 16)
            st.inc(sync.dma_start(out=dab[:, 0:TB], in_=abuf[:]), 16)
            st.inc(sync.dma_start(out=dab[:, TB:2 * TB], in_=bbuf[:]), 16)
            st.inc(sync.dma_start(out=dab[:, 2 * TB:3 * TB], in_=Hbuf[:]), 16)
            st.inc(sync.dma_start(out=dab[:, 3 * TB:4 * TB], in_=gobuf[:]), 16)
            st.inc(sync.dma_start(out=dab[:, 4 * TB:4 * TB + 16], in_=kcar[:]), 16)
            st.inc(sync.dma_start(out=dab[:, 4 * TB + 16:4 * TB + 32], in_=hcar[:]), 16)
            st.inc(sync.dma_start(out=dxn[:, 0:TB] if False else dab[:, 3 * TB:4 * TB], in_=C.glob["dbg_ks"][:]), 16)

    es2.close()


def emit_phase_retention(C, src, dst):
    nc, block, T, S = C.nc, C.block, C.T, C.S
    NT = min(512, S)
    nck = S // NT
    NSEQ = T // S
    QG = 256
    NG = S // QG
    NTT = NT // 128
    st = C.st
    st0 = st.n
    (pxl, pcs, psm, pvp, psq, pxh, pkm, pkx, prm, pqm, pqx, prm2, pvm, pvv,
     ptl, pwl, pq2, pxp, ps1m, pste, pomm, pox, pqst) = C.phase_sems(
        ["a", "b", "c", "d", "e", "f", "g", "h", "i", "j", "k", "l", "m",
         "n", "o", "p", "q", "r", "s", "t", "u", "v", "w"])
    pwl = C.fresh()
    pkst = C.fresh()
    pvld = C.fresh()

    es2 = contextlib.ExitStack()
    sb = lambda n, sh, dt: es2.enter_context(nc.sbuf_tensor(f"e{C.pid}_{n}", sh, dt))
    ps = lambda n, sh: es2.enter_context(nc.psum_tensor(f"e{C.pid}_{n}", sh, F32))
    xr = [sb("xr0", [128, NCH * NT], F32)]
    xh = sb("xh", [128, NCH * NT], BF16)
    xbf = [sb(f"xbf{j}", [128, NT], BF16) for j in range(2)]
    sqbf = [sb(f"sqbf{j}", [128, NT], BF16) for j in range(2)]
    mneg = sb("mneg", [128, NT], F32)
    rstd = sb("rstd", [128, NT], F32)
    tmp1 = sb("tmp1", [128, NT], F32)
    tmp2 = sb("tmp2", [128, NT], F32)
    wqE = sb("wqE", [128, NCH * D], BF16)
    wkE = sb("wkE", [128, NCH * D], BF16)
    wvE = sb("wvE", [128, NCH * D], BF16)
    rob = sb("rob", [128, 128], BF16)
    glf = sb("glf", [128, 128], F32)
    glt = sb("glt", [128, 128], F32)
    kr = sb("kr", [128, NT], BF16)
    qr = sb("qr", [128, NT], BF16)
    rrk = sb("rrk", [128, NT], BF16)
    rrq = sb("rrq", [128, NT], BF16)
    qxs = [sb(f"qxs{j}", [128, NT], BF16) for j in range(2)]
    tblq = [sb("tq0", [128, 2 * NT], BF16)]
    tblk = [sb("tk0", [128, 2 * NT], BF16)]
    kxs = [sb(f"kxs{j}", [128, NT], BF16) for j in range(2)]
    kslab = [sb(f"ksl{j}", [128, NCH * 128], BF16) for j in range(2)]
    vtm = sb("vtm", [128, (S // 128) * D], BF16)
    qx2 = [sb(f"qx2{j}", [128, NCH * QG], BF16) for j in range(2)]
    stb = sb("stb", [128, (S // 128) * QG], BF16)
    xp = [sb(f"xp{j}", [128, NCH * QG], F32) for j in range(2)]
    psS = ps("psS", [128, NT])
    psQt = ps("psQt", [128, NT])
    psP = ps("psP", [128, NT])
    psR = ps("psR", [128, NT])
    psP2 = ps("psP2", [128, NT])
    psR2 = ps("psR2", [128, NT])
    psV = ps("psV", [128, D])
    psT = psP[:, 0:QG]
    psO = psP2[:, 0:QG]
    qxd = C.dram["qx"]
    kxv = C.dram["kx2"]
    GCS = nck * NCH
    PSEQ = NG * (NG + 1)

    def pair_idx(sq_, g2_, k_):
        return sq_ * PSEQ + g2_ * (g2_ + 1) + k_

    @block.gpsimd
    def _(g):
        g.wait_ge(st.h, st0)
        for nm, buf in [("wE_q", wqE), ("wE_k", wkE), ("wE_v", wvE)]:
            for k in range(NCH):
                pwl.inc(g.dma_start(out=buf[:, D * k:D * (k + 1)],
                                    in_=C.dram[nm][128 * k:128 * k + 128, :]), 16)

    @block.sync
    def _(sync):
        sync.wait_ge(st.h, st0)
        ptl.inc(sync.dma_start(out=glf[:], in_=C.dram["gloc"]), 16)
        ptl.inc(sync.dma_start(out=glt[:], in_=C.dram["gloc_tri"]), 16)
        ptl.inc(sync.dma_start(out=rob[:], in_=C.dram["rotm"]), 16)
        for sq in range(NSEQ):
            for i in range(nck):
                blk = sq * nck + i
                if blk >= 1:
                    pxh.wait(sync, blk)
                for c in range(NCH):
                    pxl.inc(sync.dma_start(out=xr[0][:, c * NT:(c + 1) * NT],
                                           in_=src[128 * c:128 * c + 128, blk * NT:(blk + 1) * NT]), 16)
                for c in range(NCH):
                    gc = blk * NCH + c
                    if gc >= 1:
                        pkx.wait(sync, 2 * gc)
                        pqx.wait(sync, gc)
                    tb = tblq[0]
                    ptl.inc(sync.dma_start(out=tb[:, 0:NT], in_=C.dram["tq_cos"][128 * c:128 * c + 128, i * NT:(i + 1) * NT]), 16)
                    ptl.inc(sync.dma_start(out=tb[:, NT:2 * NT], in_=C.dram["tq_sin"][128 * c:128 * c + 128, i * NT:(i + 1) * NT]), 16)
                    tb = tblk[0]
                    ptl.inc(sync.dma_start(out=tb[:, 0:NT], in_=C.dram["tk_cos"][128 * c:128 * c + 128, i * NT:(i + 1) * NT]), 16)
                    ptl.inc(sync.dma_start(out=tb[:, NT:2 * NT], in_=C.dram["tk_sin"][128 * c:128 * c + 128, i * NT:(i + 1) * NT]), 16)
                    pqx.wait(sync, gc + 1)
                    pqst.inc(sync.dma_start(out=qxd[128 * c:128 * c + 128, blk * NT:(blk + 1) * NT],
                                            in_=qxs[gc % 2][:]), 16)
                    pkx.wait(sync, 2 * gc + 2)
                    for t_ in range(NT // 128):
                        r0 = (blk * (NT // 128) + t_) * 128
                        pkst.inc(sync.dma_start(
                            out=kxv[r0:r0 + 128, 128 * c:128 * (c + 1)],
                            in_=kxs[gc % 2][:, 128 * t_:128 * (t_ + 1)]), 16)
            for g2 in range(NG):
                gg = sq * NG + g2
                t0 = sq * S + g2 * QG
                if gg >= 2:
                    pomm.wait(sync, NCH * (gg - 1))
                if gg >= 1:
                    sync.wait_ge(st.h, st0 + 16 * NCH * gg)
                for c in range(NCH):
                    pq2.inc(sync.dma_start(out=qx2[gg % 2][:, c * QG:(c + 1) * QG],
                                           in_=qxd[128 * c:128 * c + 128, t0:t0 + QG]), 16)
                    pxp.inc(sync.dma_start(out=xp[0][:, c * QG:(c + 1) * QG],
                                           in_=src[128 * c:128 * c + 128, t0:t0 + QG]), 16)
                nk_ = 2 * g2 + 2
                for k_ in range(nk_):
                    gk_ = pair_idx(sq, g2, k_)
                    if gk_ >= 2:
                        ps1m.wait(sync, gk_ - 1)
                    r0 = (sq * (S // 128) + k_) * 128
                    pvld.inc(sync.dma_start(out=kslab[gk_ % 2][:], in_=kxv[r0:r0 + 128, :]), 16)
                pox.wait(sync, NCH * (gg + 1))
                for c in range(NCH):
                    st.inc(sync.dma_start(out=dst[128 * c:128 * c + 128, t0:t0 + QG],
                                          in_=xp[0][:, c * QG:(c + 1) * QG]), 16)

    @block.scalar
    def _(s):
        for sq in range(NSEQ):
            for i in range(nck):
                blk = sq * nck + i
                pxl.wait(s, 16 * NCH * (blk + 1))
                for c in range(NCH):
                    idx = blk * NCH + c
                    if idx >= 2:
                        psm.wait(s, 2 * (idx - 1))
                    srcap = xr[0][:, c * NT:(c + 1) * NT]
                    pcs.inc(s.activation(xbf[idx % 2][:], srcap, AF.Copy), 1)
                    pcs.inc(s.activation(sqbf[idx % 2][:], srcap, AF.Square), 1)
                pvp.wait(s, blk + 1)
                psq.inc(s.activation(tmp2[:], rstd[:], AF.Sqrt,
                                     bias=C.glob["cb"][:, C_LNEPS:C_LNEPS + 1]), 1)

    @block.tensor
    def _(t):
        t.wait_ge(C.sem("ldglob").h, 48)
        pwl.wait(t, 16 * 3 * NCH)
        ptl.wait(t, 48)
        for sq in range(NSEQ):
            for i in range(nck):
                blk = sq * nck + i
                if sq >= 1 and i == 0:
                    pste.wait(t, 2 * PSEQ * sq)
                    pox.wait(t, NCH * NG * sq)
                if blk >= 1:
                    pvp.wait(t, blk)
                for c in range(NCH):
                    idx = blk * NCH + c
                    pcs.wait(t, 2 * (idx + 1))
                    t.matmul(psS[:], C.glob["ones"][:], xbf[idx % 2][:], start=(c == 0), stop=(c == NCH - 1))
                    inst = t.matmul(psQt[:], C.glob["ones"][:], sqbf[idx % 2][:], start=(c == 0), stop=(c == NCH - 1))
                    psm.inc(inst, 2)
                pxh.wait(t, blk + 1)
                for c in range(NCH):
                    gc = blk * NCH + c
                    if gc >= 1:
                        pkx.wait(t, 2 * gc - 1)
                    for k in range(NCH):
                        inst = t.matmul(psP[:], wkE[:, D * k + 128 * c: D * k + 128 * (c + 1)],
                                        xh[:, k * NT:(k + 1) * NT], start=(k == 0), stop=(k == NCH - 1))
                    pkm.inc(inst, 1)
                    if gc >= 1:
                        pkx.wait(t, 2 * gc)
                    pkx.wait(t, 2 * gc + 1)
                    prm.inc(t.matmul(psR[:], rob[:], kr[:], start=True, stop=True), 1)
                    if gc >= 1:
                        pqx.wait(t, gc)
                    for k in range(NCH):
                        inst = t.matmul(psP2[:], wqE[:, D * k + 128 * c: D * k + 128 * (c + 1)],
                                        xh[:, k * NT:(k + 1) * NT], start=(k == 0), stop=(k == NCH - 1))
                    pqm.inc(inst, 1)
                    prm2.wait(t, 2 * gc + 1)
                    prm2.inc(t.matmul(psR2[:], rob[:], qr[:], start=True, stop=True), 1)
                for tt in range(NTT):
                    gt = blk * NTT + tt
                    if gt >= 1:
                        pvv.wait(t, gt)
                    for half in range(2):
                        for k in range(NCH):
                            inst = t.matmul(psV[:, 512 * half:512 * half + 512],
                                            xh[:, k * NT + tt * 128: k * NT + (tt + 1) * 128],
                                            wvE[:, D * k + 512 * half:D * k + 512 * half + 512],
                                            start=(k == 0), stop=(k == NCH - 1))
                    pvm.inc(inst, 1)
            pkx.wait(t, 2 * GCS * (sq + 1))
            pqx.wait(t, GCS * (sq + 1))
            pvv.wait(t, NTT * nck * (sq + 1))
            for g2 in range(NG):
                gg = sq * NG + g2
                nk = 2 * g2 + 2
                pq2.wait(t, 16 * NCH * (gg + 1))
                for k in range(nk):
                    gk = pair_idx(sq, g2, k)
                    if gk >= 1:
                        pste.wait(t, 2 * gk)
                    pvld.wait(t, 16 * (gk + 1))
                    for c in range(NCH):
                        inst = t.matmul(psT, kslab[gk % 2][:, 128 * c:128 * (c + 1)],
                                        qx2[gg % 2][:, c * QG:(c + 1) * QG],
                                        start=(c == 0), stop=(c == NCH - 1))
                    ps1m.inc(inst, 1)
                pste.wait(t, 2 * pair_idx(sq, g2, nk - 1) + 2)
                for c in range(NCH):
                    go = gg * NCH + c
                    if go >= 1:
                        pox.wait(t, go)
                    for k in range(nk):
                        inst = t.matmul(psO, vtm[:, k * D + 128 * c: k * D + 128 * (c + 1)],
                                        stb[:, k * QG:(k + 1) * QG],
                                        start=(k == 0), stop=(k == nk - 1))
                    pomm.inc(inst, 1)

    @block.vector
    def _(v):
        for sq in range(NSEQ):
            for i in range(nck):
                blk = sq * nck + i
                if sq >= 1 and i == 0:
                    ps1m.wait(v, PSEQ * sq)
                    pomm.wait(v, NCH * NG * sq)
                psm.wait(v, 2 * NCH * (blk + 1))
                v.tensor_scalar(mneg[:], psS[:], -1.0 / D, None, ALU.mult)
                v.tensor_tensor(tmp1[:], mneg[:], mneg[:], ALU.mult)
                pvp.inc(v.scalar_tensor_tensor(rstd[:], psQt[:], 1.0 / D, tmp1[:], ALU.mult, ALU.subtract), 1)
                psq.wait(v, blk + 1)
                v.reciprocal(rstd[:], tmp2[:])
                for c in range(NCH):
                    xsl_ = xr[0][:, c * NT:(c + 1) * NT]
                    v.tensor_tensor(tmp1[:], xsl_, mneg[:], ALU.add)
                    inst = v.tensor_tensor(xh[:, c * NT:(c + 1) * NT], tmp1[:], rstd[:], ALU.mult)
                pxh.inc(inst, 1)
                for c in range(NCH):
                    gc = blk * NCH + c
                    ptl.wait(v, 16 * (3 + 4 * (gc + 1)))
                    pkm.wait(v, gc + 1)
                    pkx.inc(v.tensor_copy(kr[:], psP[:]), 1)
                    prm.wait(v, gc + 1)
                    v.tensor_copy(rrk[:], psR[:])
                    if gc >= 2:
                        pkst.wait(v, 16 * (NT // 128) * (gc - 1))
                    v.tensor_tensor(tmp1[:], kr[:], tblk[0][:, 0:NT], ALU.mult)
                    v.tensor_tensor(tmp2[:], rrk[:], tblk[0][:, NT:2 * NT], ALU.mult)
                    pkx.inc(v.tensor_tensor(kxs[gc % 2][:], tmp1[:], tmp2[:], ALU.add), 1)
                    pqm.wait(v, gc + 1)
                    prm2.inc(v.tensor_copy(qr[:], psP2[:]), 1)
                    prm2.wait(v, 2 * (gc + 1))
                    v.tensor_copy(rrq[:], psR2[:])
                    if gc >= 2:
                        pqst.wait(v, 16 * (gc - 1))
                    v.tensor_tensor(tmp1[:], qr[:], tblq[0][:, 0:NT], ALU.mult)
                    v.tensor_tensor(tmp2[:], rrq[:], tblq[0][:, NT:2 * NT], ALU.mult)
                    pqx.inc(v.tensor_tensor(qxs[gc % 2][:], tmp1[:], tmp2[:], ALU.add), 1)
                for tt in range(NTT):
                    gt = blk * NTT + tt
                    pvm.wait(v, gt + 1)
                    pvv.inc(v.tensor_copy(vtm[:, (i * NTT + tt) * D:(i * NTT + tt + 1) * D], psV[:]), 1)
            for g2 in range(NG):
                gg = sq * NG + g2
                nk = 2 * g2 + 2
                for k in range(nk):
                    gk = pair_idx(sq, g2, k)
                    ps1m.wait(v, gk + 1)
                    for sub in range(QG // 128):
                        qt = 2 * g2 + sub
                        dsl = stb[:, k * QG + 128 * sub: k * QG + 128 * (sub + 1)]
                        if k > qt:
                            inst = v.memset(dsl, 0.0)
                        else:
                            gl = glt if k == qt else glf
                            sc_ = float(GAMMA ** (128 * (qt - k)))
                            inst = v.scalar_tensor_tensor(dsl, psT[:, 128 * sub:128 * (sub + 1)],
                                                          sc_, gl[:], ALU.mult, ALU.mult)
                        pste.inc(inst, 1)
                for c in range(NCH):
                    go = gg * NCH + c
                    pomm.wait(v, go + 1)
                    pxp.wait(v, 16 * NCH * (gg + 1))
                    xsl_ = xp[0][:, c * QG:(c + 1) * QG]
                    pox.inc(v.tensor_tensor(xsl_, xsl_, psO, ALU.add), 1)

    es2.close()


def emit_phase_moe(C, src, dst):
    nc, block, T = C.nc, C.block, C.T
    NT = min(512, T)
    CK = min(1024, T)
    SUB = CK // NT
    nchunk = T // CK
    NTT = CK // 128
    st = C.st
    st0 = st.n
    (pxl, pcs, psm, pvp, psq, pxh, prt, prk, ptr, pcp, pcb, pwg, pwu, pwd,
     phg, phe, pdn, pev, pgl, hgmm, pbc, pdm, pcm, pgs) = C.phase_sems(
        ["xl", "cs", "sm", "vp", "sq", "xh", "rt", "rk", "tr", "cp", "cb",
         "wg", "wu", "wd", "hg", "he", "dn", "ev", "gl", "hgmm", "bc", "dm", "cm", "gs"])
    pwg, pwu, pwd, pgl = C.fresh(), C.fresh(), C.fresh(), C.fresh()

    es2 = contextlib.ExitStack()
    sb = lambda n, sh, dt: es2.enter_context(nc.sbuf_tensor(f"m{C.pid}_{n}", sh, dt))
    ps = lambda n, sh, dt: es2.enter_context(nc.psum_tensor(f"m{C.pid}_{n}", sh, dt))
    y = sb("y", [128, NCH * CK], F32)
    xh = sb("xh", [128, NCH * CK], BF16)
    hbuf = sb("h", [128, NCH * CK], BF16)
    wgS = [sb(f"wgS{j}", [128, NCH * D], BF16) for j in range(2)]
    wuS = [sb(f"wuS{j}", [128, NCH * D], BF16) for j in range(2)]
    wdS = [sb(f"wdS{j}", [128, NCH * D], BF16) for j in range(2)]
    gtb = sb("gtb", [128, NCH * NEXP], BF16)
    ebr = sb("ebr", [128, NEXP], F32)
    ohb = sb("ohb", [128, NEXP * 128], BF16)
    ltm = sb("ltm", [128, NEXP * NEXP], F32)
    xbf = [sb(f"xbf{i}", [128, NT], BF16) for i in range(2)]
    sqbf = [sb(f"sqbf{i}", [128, NT], BF16) for i in range(2)]
    mnegM = sb("mneg", [128, CK], F32)
    rstdM = sb("rstd", [128, CK], F32)
    tmp1 = sb("tmp1", [128, NT], F32)
    tmp2 = sb("tmp2", [128, NT], F32)
    tsil = sb("tsil", [128, NT], F32)
    th1 = sb("th1", [128, NT], F32)
    scb = sb("scb", [128, NTT * NEXP], F32)
    rk1 = sb("rk1", [128, NEXP * NEXP], F32)
    rk2 = sb("rk2", [128, NEXP * NEXP], F32)
    rks = sb("rks", [128, 4 * NEXP], F32)
    cmbb = sb("cmbb", [128, NEXP], BF16)
    cmbP = sb("cmbP", [128, CK], BF16)
    cmbr = sb("cmbr", [128, CK], F32)
    ps0 = ps("ps0", [128, NT], F32)
    ps1 = ps("ps1", [128, NT], F32)
    ps2 = ps("ps2", [128, NT], F32)
    ps3 = ps("ps3", [128, NT], F32)
    psD = [ps(f"psD{i}", [128, NT], F32) for i in range(2)]
    ps_mg = ps("ps_mg", [128, NT], F32)
    ps_tr = ps("ps_tr", [16, 128], BF16)
    bufs = {"xbf": xbf, "sqbf": sqbf, "ps_s": ps0, "ps_q": ps1,
            "mneg": tmp1, "rstd": tmp2, "tmp1": th1}
    NE = NEXP + 1

    @block.gpsimd
    def _(g):
        g.wait_ge(st.h, st0)
        for k in range(NCH):
            pgl.inc(g.dma_start(out=gtb[:, NEXP * k:NEXP * (k + 1)],
                                in_=C.dram["gateT"][128 * k:128 * k + 128, :]), 16)
        for i in range(nchunk):
            for e in range(NE):
                base = NE * i + e
                if base >= 2:
                    hgmm.wait(g, NCH * SUB * (base - 1))
                srcg = C.dram["wg_t"][e] if e < NEXP else C.dram["sg_t"]
                srcu = C.dram["wu_t"][e] if e < NEXP else C.dram["su_t"]
                for k in range(NCH):
                    pwg.inc(g.dma_start(out=wgS[base % 2][:, D * k:D * (k + 1)],
                                        in_=srcg[128 * k:128 * k + 128, :]), 16)
                    pwu.inc(g.dma_start(out=wuS[base % 2][:, D * k:D * (k + 1)],
                                        in_=srcu[128 * k:128 * k + 128, :]), 16)
                if base >= 2:
                    pdm.wait(g, NCH * SUB * (base - 1))
                srcd = C.dram["wd_t"][e] if e < NEXP else C.dram["sd_t"]
                for k in range(NCH):
                    pwd.inc(g.dma_start(out=wdS[base % 2][:, D * k:D * (k + 1)],
                                        in_=srcd[128 * k:128 * k + 128, :]), 16)

    @block.sync
    def _(sync):
        sync.wait_ge(st.h, st0)
        pgs.inc(sync.dma_start(out=ebr[:], in_=C.dram["ebias"]), 16)
        pgs.inc(sync.dma_start(out=ohb[:], in_=C.dram["onehots"]), 16)
        pgs.inc(sync.dma_start(out=ltm[:], in_=C.dram["ltmask"]), 16)
        for i in range(nchunk):
            if i >= 1:
                sync.wait_ge(st.h, st0 + 16 * NCH * i)
            for c in range(NCH):
                pxl.inc(sync.dma_start(out=y[:, c * CK:(c + 1) * CK],
                                       in_=src[128 * c:128 * c + 128, i * CK:(i + 1) * CK]), 16)
            pev.wait(sync, NCH * SUB * NE * (i + 1))
            for c in range(NCH):
                st.inc(sync.dma_start(out=dst[128 * c:128 * c + 128, i * CK:(i + 1) * CK],
                                      in_=y[:, c * CK:(c + 1) * CK]), 16)

    @block.scalar
    def _(s):
        for i in range(nchunk):
            for sub in range(SUB):
                pxl.wait(s, 16 * NCH * (i + 1))
                for c in range(NCH):
                    idx = (i * SUB + sub) * NCH + c
                    if idx >= 2:
                        psm.wait(s, 2 * (idx - 1))
                    srcap = y[:, c * CK + sub * NT: c * CK + (sub + 1) * NT]
                    pcs.inc(s.activation(xbf[idx % 2][:], srcap, AF.Copy), 1)
                    pcs.inc(s.activation(sqbf[idx % 2][:], srcap, AF.Square), 1)
                subg = i * SUB + sub
                pvp.wait(s, subg + 1)
                psq.inc(s.activation(tmp2[:], rstdM[:, sub * NT:(sub + 1) * NT], AF.Sqrt,
                                     bias=C.glob["cb"][:, C_LNEPS:C_LNEPS + 1]), 1)
            for tt in range(NTT):
                prt.wait(s, NTT * i + tt + 1)
                prk.inc(s.activation(scb[:, NEXP * tt:NEXP * (tt + 1)],
                                     ps_mg[:, 0:NEXP], AF.Sigmoid), 1)
            for e in range(NE):
                for ic in range(NCH):
                    for sub in range(SUB):
                        g = ((i * NE + e) * NCH + ic) * SUB + sub
                        if g >= 1:
                            phe.wait(s, 2 * g)
                        hgmm.wait(s, g + 1)
                        psA = [ps0, ps2][(ic * SUB + sub) % 2]
                        phe.inc(s.activation(tsil[:], psA[:], AF.Silu), 1)

    @block.tensor
    def _(t):
        t.wait_ge(C.sem("ldglob").h, 48)
        pgl.wait(t, 16 * NCH)
        pgs.wait(t, 48)
        for i in range(nchunk):
            for sub in range(SUB):
                subg = i * SUB + sub
                if subg >= 1:
                    pvp.wait(t, subg)
                if i >= 1 and sub == 0:
                    phe.wait(t, 2 * NE * NCH * SUB * i)
                    pev.wait(t, NCH * SUB * NE * i)
                for c in range(NCH):
                    idx = subg * NCH + c
                    pcs.wait(t, 2 * (idx + 1))
                    t.matmul(ps0[:], C.glob["ones"][:], xbf[idx % 2][:], start=(c == 0), stop=(c == NCH - 1))
                    inst = t.matmul(ps1[:], C.glob["ones"][:], sqbf[idx % 2][:], start=(c == 0), stop=(c == NCH - 1))
                    psm.inc(inst, 2)
            for tt in range(NTT):
                pxh.wait(t, SUB * i + (tt * 128) // NT + 1)
                if NTT * i + tt >= 1:
                    prk.wait(t, NTT * i + tt)
                for k in range(NCH):
                    inst = t.matmul(ps_mg[:, 0:NEXP],
                                    xh[:, k * CK + tt * 128: k * CK + (tt + 1) * 128],
                                    gtb[:, NEXP * k:NEXP * (k + 1)],
                                    start=(k == 0), stop=(k == NCH - 1))
                prt.inc(inst, 1)
                pcp.wait(t, NTT * i + tt)
                pcm.wait(t, NTT * i + tt + 1)
                ptr.inc(t.transpose(ps_tr[:], cmbb[:], C.glob["ident"][:]), 1)
            for e in range(NE):
                base = NE * i + e
                if e < NEXP:
                    bcg0 = (i * NEXP + e) * SUB
                    pcp.wait(t, NTT * (i + 1))
                    for sub in range(SUB):
                        if bcg0 + sub >= 1:
                            pcb.wait(t, bcg0 + sub)
                        inst = t.matmul(ps_mg[:], ohb[:, 128 * e:128 * (e + 1)],
                                        cmbP[:, sub * NT:(sub + 1) * NT], start=True, stop=True)
                        pbc.inc(inst, 1)
                pwg.wait(t, 16 * NCH * (base + 1))
                pwu.wait(t, 16 * NCH * (base + 1))
                for ic in range(NCH):
                    for sub in range(SUB):
                        g = (base * NCH + ic) * SUB + sub
                        pp = (ic * SUB + sub) % 2
                        psA, psB = [ps0, ps2][pp], [ps1, ps3][pp]
                        if g >= 2:
                            phe.wait(t, 2 * (g - 1))
                        if e == 0 and ic == 0 and sub <= 1:
                            pvp.wait(t, SUB * (i + 1))
                        xsl = lambda k, sub=sub: xh[:, k * CK + sub * NT: k * CK + (sub + 1) * NT]
                        for k in range(NCH):
                            t.matmul(psA[:], wgS[base % 2][:, D * k + 128 * ic: D * k + 128 * (ic + 1)],
                                     xsl(k), start=(k == 0), stop=(k == NCH - 1))
                        for k in range(NCH):
                            inst = t.matmul(psB[:], wuS[base % 2][:, D * k + 128 * ic: D * k + 128 * (ic + 1)],
                                            xsl(k), start=(k == 0), stop=(k == NCH - 1))
                        hgmm.inc(inst, 1)
                pwd.wait(t, 16 * NCH * (base + 1))
                phe.wait(t, 2 * ((base + 1) * NCH * SUB))
                for dch in range(NCH):
                    for sub in range(SUB):
                        gd = (base * NCH + dch) * SUB + sub
                        pD = psD[(dch * SUB + sub) % 2]
                        if gd >= 2:
                            pev.wait(t, gd - 1)
                        for ic in range(NCH):
                            inst = t.matmul(pD[:], wdS[base % 2][:, D * ic + 128 * dch: D * ic + 128 * (dch + 1)],
                                            hbuf[:, ic * CK + sub * NT: ic * CK + (sub + 1) * NT],
                                            start=(ic == 0), stop=(ic == NCH - 1))
                        pdm.inc(inst, 1)

    @block.vector
    def _(v):
        v.wait_ge(st.h, st0)
        v.memset(cmbP[:], 0.0)
        for i in range(nchunk):
            for sub in range(SUB):
                subg = i * SUB + sub
                psm.wait(v, 2 * NCH * (subg + 1))
                mn = mnegM[:, sub * NT:(sub + 1) * NT]
                rs = rstdM[:, sub * NT:(sub + 1) * NT]
                v.tensor_scalar(mn, ps0[:], -1.0 / D, None, ALU.mult)
                v.tensor_tensor(th1[:], mn, mn, ALU.mult)
                pvp.inc(v.scalar_tensor_tensor(rs, ps1[:], 1.0 / D, th1[:], ALU.mult, ALU.subtract), 1)
                psq.wait(v, subg + 1)
                v.reciprocal(rs, tmp2[:])
                for c in range(NCH):
                    sl = y[:, c * CK + sub * NT: c * CK + (sub + 1) * NT]
                    v.tensor_tensor(th1[:], sl, mn, ALU.add)
                    inst = v.tensor_tensor(xh[:, c * CK + sub * NT: c * CK + (sub + 1) * NT],
                                           th1[:], rs, ALU.mult)
                pxh.inc(inst, 1)
            for tt in range(NTT):
                prk.wait(v, NTT * i + tt + 1)
                sc = scb[:, NEXP * tt:NEXP * (tt + 1)]
                sel = rks[:, 0:NEXP]
                C.vdrain(v, v.tensor_tensor(sel, sc, ebr[:], ALU.add))
                selE = sel.unsqueeze(1).broadcast_to([128, NEXP, NEXP])
                v.tensor_tensor(rk1[:].rearrange("p (a b) -> p a b", a=NEXP),
                                sel.unsqueeze(1).broadcast_to([128, NEXP, NEXP]),
                                sel.unsqueeze(2).broadcast_to([128, NEXP, NEXP]), ALU.is_gt)
                v.tensor_tensor(rk2[:].rearrange("p (a b) -> p a b", a=NEXP),
                                sel.unsqueeze(1).broadcast_to([128, NEXP, NEXP]),
                                sel.unsqueeze(2).broadcast_to([128, NEXP, NEXP]), ALU.is_equal)
                v.tensor_tensor(rk2[:], rk2[:], ltm[:], ALU.mult)
                v.tensor_tensor(rk1[:], rk1[:], rk2[:], ALU.add)
                rank = rks[:, NEXP:2 * NEXP]
                C.vdrain(v, v.reduce_sum(rank, rk1[:].rearrange("p (a b) -> p a b", a=NEXP), axis=AX.X))
                mask = rks[:, 2 * NEXP:3 * NEXP]
                C.vdrain(v, v.tensor_scalar(mask, rank, 7.5, None, ALU.is_lt))
                msc = rks[:, 3 * NEXP:4 * NEXP]
                C.vdrain(v, v.tensor_tensor(msc, mask, sc, ALU.mult))
                C.vdrain(v, v.reduce_sum(rks[:, 0:1], msc, axis=AX.X))
                C.vdrain(v, v.reciprocal(rks[:, 1:2], rks[:, 0:1]))
                pcm.inc(v.tensor_scalar(cmbb[:], msc, rks[:, 1:2], ROUTE_SCALE, ALU.mult, ALU.mult), 1)
                ptr.wait(v, NTT * i + tt + 1)
                pcp.inc(v.tensor_copy(cmbP[0:16, tt * 128:(tt + 1) * 128], ps_tr[:]), 1)
            for e in range(NE):
                base = NE * i + e
                if e < NEXP:
                    bcg0 = (i * NEXP + e) * SUB
                    for sub in range(SUB):
                        pbc.wait(v, bcg0 + sub + 1)
                        pcb.inc(v.tensor_copy(cmbr[:, sub * NT:(sub + 1) * NT], ps_mg[:]), 1)
                if base >= 1:
                    pdm.wait(v, base * NCH * SUB)
                for ic in range(NCH):
                    for sub in range(SUB):
                        g = (base * NCH + ic) * SUB + sub
                        pp = (ic * SUB + sub) % 2
                        psB = [ps1, ps3][pp]
                        phe.wait(v, 2 * g + 1)
                        v.tensor_tensor(th1[:], tsil[:], psB[:], ALU.mult)
                        hsl = hbuf[:, ic * CK + sub * NT: ic * CK + (sub + 1) * NT]
                        if e < NEXP:
                            inst = v.tensor_tensor(hsl, th1[:], cmbr[:, sub * NT:(sub + 1) * NT], ALU.mult)
                        else:
                            inst = v.tensor_copy(hsl, th1[:])
                        phe.inc(inst, 1)
                if getattr(C, "dbg_moe", False) and i == 0 and e == 0:
                    v.tensor_copy(C.glob["dbg_hf"][:, 0:NCH * CK], hbuf[:])
                    v.tensor_copy(C.glob["dbg_cm"][:, 0:CK], cmbr[:])
                for dch in range(NCH):
                    for sub in range(SUB):
                        gd = (base * NCH + dch) * SUB + sub
                        pdm.wait(v, gd + 1)
                        pD = psD[(dch * SUB + sub) % 2]
                        sl = y[:, dch * CK + sub * NT: dch * CK + (sub + 1) * NT]
                        pev.inc(v.tensor_tensor(sl, sl, pD[:], ALU.add), 1)

    if getattr(C, "dbg_moe", False):
        dscb = nc.dram_tensor("dbg_scb", [128, NTT * NEXP], F32, kind="ExternalOutput").ap()
        dcmbP = nc.dram_tensor("dbg_cmbP", [128, CK], F32, kind="ExternalOutput").ap()
        dxh = nc.dram_tensor("dbg_xh", [128, NCH * CK], F32, kind="ExternalOutput").ap()
        dh = nc.dram_tensor("dbg_h", [128, NCH * CK], F32, kind="ExternalOutput").ap()
        cmbPf = sb("cmbPf", [128, CK], F32)
        xhf = sb("xhf", [128, NCH * CK], F32)
        dh0 = nc.dram_tensor("dbg_h0", [128, NCH * CK], F32, kind="ExternalOutput").ap()
        dcm0 = nc.dram_tensor("dbg_cm0", [128, CK], F32, kind="ExternalOutput").ap()

        @block.vector
        def _(v):
            pev.wait(v, NCH * SUB * NE * nchunk)
            v.tensor_copy(cmbPf[:], cmbP[:])
            C.sem(f"p{C.pid}_dbg").inc(v.tensor_copy(xhf[:], xh[:]), 1)

        @block.sync
        def _(sync):
            sync.wait_ge(C.sem(f"p{C.pid}_dbg").h, 1)
            st.inc(sync.dma_start(out=dscb, in_=scb[:]), 16)
            st.inc(sync.dma_start(out=dcmbP, in_=cmbPf[:]), 16)
            st.inc(sync.dma_start(out=dxh, in_=xhf[:]), 16)
            st.inc(sync.dma_start(out=dh, in_=C.glob["dbg_hf"][:, 0:NCH * CK]), 16)
            st.inc(sync.dma_start(out=dh0, in_=C.glob["dbg_hf"][:, 0:NCH * CK]), 16)
            st.inc(sync.dma_start(out=dcm0, in_=C.glob["dbg_cm"][:, 0:CK]), 16)

    es2.close()


def emit_phase_attn_g(C, src, dst):
    nc, block, T, S = C.nc, C.block, C.T, C.S
    NT = min(512, S)
    nck = S // NT
    NSEQ = T // S
    NH, HD = 16, 64
    NH2 = max(1, NT // 256)
    st = C.st
    st0 = st.n
    (pxl, pcs, psm, pvp, psq, pxh, pqm, pqk, pvm, pvv, pzf, pzs, pex, pz2m,
     pz2, pom, pox, pwl, pxp) = C.phase_sems(
        ["xl", "cs", "sm", "vp", "sq", "xh", "qm", "qk", "vm", "vv", "zf",
         "zs", "ex", "z2m", "z2", "om", "ox", "wl", "xp"])
    pwl = C.fresh()

    es2 = contextlib.ExitStack()
    sb = lambda n, sh, dt: es2.enter_context(nc.sbuf_tensor(f"g{C.pid}_{n}", sh, dt))
    ps = lambda n, sh: es2.enter_context(nc.psum_tensor(f"g{C.pid}_{n}", sh, F32))
    xr = [sb("xr0", [128, NCH * NT], F32)]
    xp = [sb(f"xp{j}", [128, NCH * 256], F32) for j in range(2)]
    xh = sb("xh", [128, NCH * NT], BF16)
    xbf = [sb(f"xbf{j}", [128, NT], BF16) for j in range(2)]
    sqbf = [sb(f"sqbf{j}", [128, NT], BF16) for j in range(2)]
    mneg = sb("mneg", [128, NT], F32)
    rstd = sb("rstd", [128, NT], F32)
    tmp1 = sb("tmp1", [128, NT], F32)
    tmp2 = sb("tmp2", [128, NT], F32)
    wq = sb("wq", [128, NCH * D], BF16)
    wk = sb("wk", [128, NCH * D], BF16)
    wv = sb("wv", [128, NCH * D], BF16)
    wo = sb("wo", [128, NCH * D], BF16)
    qb = sb("qb", [128, D], BF16)
    kb = sb("kb", [128, D], BF16)
    vbuf = sb("vbuf", [128, NCH * S], BF16)
    zz = sb("zz", [128, 512], F32)
    zacc = sb("zacc", [128, 512], F32)
    zz2 = sb("zz2", [128, 512], F32)
    zmx = sb("zmx", [128, 8], F32)
    zsm = sb("zsm", [128, 512], BF16)
    z2b = sb("z2b", [128, NCH * 256], BF16)
    zb0 = sb("zb0", [128, 128], BF16)
    psQ = ps("psQ", [128, D])
    psK = ps("psK", [128, D])
    psV = ps("psV", [128, NT])
    psZ = ps("psZ", [128, 512])
    psS = ps("psS", [128, NT])
    psQt = ps("psQt", [128, NT])
    z2slot = [psQ[:, 256 * j:256 * (j + 1)] for j in range(4)] + \
             [psK[:, 256 * j:256 * (j + 1)] for j in range(4)]
    oslot = [psV[:, 0:256], psS[:, 0:256]]

    @block.gpsimd
    def _(g):
        g.wait_ge(st.h, st0)
        for nm, buf in [("wG_q", wq), ("wG_k", wk), ("wG_v", wv), ("wG_o", wo)]:
            for k in range(NCH):
                pwl.inc(g.dma_start(out=buf[:, D * k:D * (k + 1)],
                                    in_=C.dram[nm][128 * k:128 * k + 128, :]), 16)

    @block.sync
    def _(sync):
        sync.wait_ge(st.h, st0)
        for sq in range(NSEQ):
            for i in range(nck):
                blk = sq * nck + i
                if blk >= 1:
                    pxh.wait(sync, blk)
                for c in range(NCH):
                    pxl.inc(sync.dma_start(out=xr[0][:, c * NT:(c + 1) * NT],
                                           in_=src[128 * c:128 * c + 128, blk * NT:(blk + 1) * NT]), 16)
            for i in range(nck):
                blk = sq * nck + i
                for half in range(NH2):
                    hh = blk * NH2 + half
                    if hh >= 2:
                        sync.wait_ge(st.h, st0 + 16 * NCH * (hh - 1))
                    for c in range(NCH):
                        pxp.inc(sync.dma_start(
                            out=xp[hh % 2][:, c * 256:(c + 1) * 256],
                            in_=src[128 * c:128 * c + 128, hh * 256:(hh + 1) * 256]), 16)
                    pox.wait(sync, NCH * (hh + 1))
                    for c in range(NCH):
                        st.inc(sync.dma_start(
                            out=dst[128 * c:128 * c + 128, hh * 256:(hh + 1) * 256],
                            in_=xp[hh % 2][:, c * 256:(c + 1) * 256]), 16)

    @block.scalar
    def _(s):
        for sq in range(NSEQ):
            for i in range(nck):
                blk = sq * nck + i
                pxl.wait(s, 16 * NCH * (blk + 1))
                for c in range(NCH):
                    idx = blk * NCH + c
                    if idx >= 2:
                        psm.wait(s, 2 * (idx - 1))
                    srcap = xr[0][:, c * NT:(c + 1) * NT]
                    pcs.inc(s.activation(xbf[idx % 2][:], srcap, AF.Copy), 1)
                    pcs.inc(s.activation(sqbf[idx % 2][:], srcap, AF.Square), 1)
                pvp.wait(s, blk + 1)
                psq.inc(s.activation(tmp2[:], rstd[:], AF.Sqrt,
                                     bias=C.glob["cb"][:, C_LNEPS:C_LNEPS + 1]), 1)
            pzs.wait(s, 3 * sq + 2)
            pex.inc(s.activation(zz2[:], zz[:], AF.Exp,
                                 scale=C.glob["cb"][:, C_G8:C_G8 + 1]), 1)

    @block.tensor
    def _(t):
        t.wait_ge(C.sem("ldglob").h, 48)
        pwl.wait(t, 16 * 4 * NCH)
        for sq in range(NSEQ):
            for i in range(nck):
                blk = sq * nck + i
                if sq >= 1 and i == 0:
                    pz2.wait(t, NCH * NH2 * nck * sq)
                    pox.wait(t, NCH * NH2 * nck * sq)
                    pzs.wait(t, 3 * sq)
                if blk >= 1:
                    pvp.wait(t, blk)
                for c in range(NCH):
                    idx = blk * NCH + c
                    pcs.wait(t, 2 * (idx + 1))
                    t.matmul(psS[:], C.glob["ones"][:], xbf[idx % 2][:], start=(c == 0), stop=(c == NCH - 1))
                    inst = t.matmul(psQt[:], C.glob["ones"][:], sqbf[idx % 2][:], start=(c == 0), stop=(c == NCH - 1))
                    psm.inc(inst, 2)
                pxh.wait(t, blk + 1)
                for c in range(NCH):
                    gv = blk * NCH + c
                    if gv >= 1:
                        pvv.wait(t, gv)
                    for k in range(NCH):
                        inst = t.matmul(psV[:], wv[:, D * k + 128 * c: D * k + 128 * (c + 1)],
                                        xh[:, k * NT:(k + 1) * NT], start=(k == 0), stop=(k == NCH - 1))
                    pvm.inc(inst, 1)
                for tt in range(NT // 128):
                    gt = blk * (NT // 128) + tt
                    if gt >= 1:
                        pqk.wait(t, 2 * gt)
                    for half in range(2):
                        for k in range(NCH):
                            t.matmul(psQ[:, 512 * half:512 * half + 512],
                                     xh[:, k * NT + tt * 128: k * NT + (tt + 1) * 128],
                                     wq[:, D * k + 512 * half:D * k + 512 * half + 512],
                                     start=(k == 0), stop=(k == NCH - 1))
                    for half in range(2):
                        for k in range(NCH):
                            inst = t.matmul(psK[:, 512 * half:512 * half + 512],
                                            xh[:, k * NT + tt * 128: k * NT + (tt + 1) * 128],
                                            wk[:, D * k + 512 * half:D * k + 512 * half + 512],
                                            start=(k == 0), stop=(k == NCH - 1))
                    pqm.inc(inst, 1)
                    pqk.wait(t, 2 * gt + 2)
                    if gt == 0:
                        t.wait_ge(C.sem(f"p{C.pid}_zb").h, 1)
                    if gt >= 1:
                        pzf.wait(t, gt)
                    t.matmul(psZ[:], zb0[:], qb[:, 0:512], start=True, stop=False)
                    for h in range(NH):
                        po = psZ[64 * (h % 2):64 * (h % 2) + 64, 64 * (h // 2):64 * (h // 2) + 64]
                        t.matmul(po, kb[:, HD * h:HD * (h + 1)], qb[:, HD * h:HD * (h + 1)],
                                 start=False, stop=False)
                    inst = t.matmul(psZ[:], zb0[:], qb[:, 0:512], start=False, stop=True)
                    C.sem(f"p{C.pid}_zm").inc(inst, 1)
            pzs.wait(t, 3 * (sq + 1))
            for i in range(nck):
                blk = sq * nck + i
                for half in range(NH2):
                    hh = blk * NH2 + half
                    if hh >= 1:
                        pz2.wait(t, NCH * hh)
                    for h in range(NH):
                        c = h // 2
                        tgt = z2slot[c][64 * (h % 2):64 * (h % 2) + 64, :]
                        inst = t.matmul(
                            tgt,
                            zsm[64 * (h % 2):64 * (h % 2) + 64, 64 * (h // 2):64 * (h // 2) + 64],
                            vbuf[64 * (h % 2):64 * (h % 2) + 64,
                                 c * S + i * NT + half * 256: c * S + i * NT + half * 256 + 256],
                            start=True, stop=True)
                    pz2m.inc(inst, 1)
                    pz2.wait(t, NCH * (hh + 1))
                    for o in range(NCH):
                        go = hh * NCH + o
                        pO = oslot[o % 2]
                        if go >= 2:
                            pox.wait(t, go - 1)
                        for k in range(NCH):
                            inst = t.matmul(pO, wo[:, D * k + 128 * o: D * k + 128 * (o + 1)],
                                            z2b[:, k * 256:(k + 1) * 256], start=(k == 0), stop=(k == NCH - 1))
                        pom.inc(inst, 1)

    @block.vector
    def _(v):
        v.wait_ge(st.h, st0)
        C.sem(f"p{C.pid}_zb").inc(v.memset(zb0[:], 0.0), 1)
        for sq in range(NSEQ):
            for i in range(nck):
                blk = sq * nck + i
                psm.wait(v, 2 * NCH * (blk + 1))
                v.tensor_scalar(mneg[:], psS[:], -1.0 / D, None, ALU.mult)
                v.tensor_tensor(tmp1[:], mneg[:], mneg[:], ALU.mult)
                pvp.inc(v.scalar_tensor_tensor(rstd[:], psQt[:], 1.0 / D, tmp1[:], ALU.mult, ALU.subtract), 1)
                psq.wait(v, blk + 1)
                v.reciprocal(rstd[:], tmp2[:])
                for c in range(NCH):
                    xsl_ = xr[0][:, c * NT:(c + 1) * NT]
                    v.tensor_tensor(tmp1[:], xsl_, mneg[:], ALU.add)
                    inst = v.tensor_tensor(xh[:, c * NT:(c + 1) * NT], tmp1[:], rstd[:], ALU.mult)
                pxh.inc(inst, 1)
                for c in range(NCH):
                    pvm.wait(v, blk * NCH + c + 1)
                    pvv.inc(v.tensor_copy(vbuf[:, c * S + i * NT: c * S + (i + 1) * NT], psV[:]), 1)
                for tt in range(NT // 128):
                    gt = blk * (NT // 128) + tt
                    pqm.wait(v, gt + 1)
                    pqk.inc(v.tensor_copy(qb[:], psQ[:]), 1)
                    pqk.inc(v.tensor_copy(kb[:], psK[:]), 1)
                    v.wait_ge(C.sem(f"p{C.pid}_zm").h, gt + 1)
                    if i == 0 and tt == 0:
                        pzf.inc(v.tensor_copy(zacc[:], psZ[:]), 1)
                    else:
                        pzf.inc(v.tensor_tensor(zacc[:], zacc[:], psZ[:], ALU.add), 1)
            pzs.inc(v.tensor_copy(zz[:], zacc[:]), 1)
            C.vdrain(v, v.reduce_max(zmx[:], zz[:].rearrange("p (a b) -> p a b", b=64), axis=AX.X))
            pzs.inc(v.tensor_tensor(zz[:].rearrange("p (a b) -> p a b", b=64),
                                    zz[:].rearrange("p (a b) -> p a b", b=64),
                                    zmx[:].unsqueeze(2).broadcast_to([128, 8, 64]), ALU.subtract), 1)
            pex.wait(v, sq + 1)
            C.vdrain(v, v.reduce_sum(zmx[:], zz2[:].rearrange("p (a b) -> p a b", b=64), axis=AX.X))
            C.vdrain(v, v.reciprocal(zmx[:], zmx[:]))
            pzs.inc(v.tensor_tensor(zsm[:].rearrange("p (a b) -> p a b", b=64),
                                    zz2[:].rearrange("p (a b) -> p a b", b=64),
                                    zmx[:].unsqueeze(2).broadcast_to([128, 8, 64]), ALU.mult), 1)
            if getattr(C, "dbg_g", False) and sq == 0:
                v.tensor_copy(C.glob["dbg_zz"][:], zacc[:])
                v.tensor_copy(C.glob["dbg_zsm"][:], zsm[:])
            for i in range(nck):
                blk = sq * nck + i
                for half in range(NH2):
                    hh = blk * NH2 + half
                    pz2m.wait(v, hh + 1)
                    for c in range(NCH):
                        pz2.inc(v.tensor_copy(z2b[:, c * 256:(c + 1) * 256], z2slot[c]), 1)
                    for o in range(NCH):
                        go = hh * NCH + o
                        pom.wait(v, go + 1)
                        pxp.wait(v, 16 * NCH * (hh + 1))
                        xsl_ = xp[hh % 2][:, o * 256:(o + 1) * 256]
                        pox.inc(v.tensor_tensor(xsl_, xsl_, oslot[o % 2], ALU.add), 1)

    es2.close()


def build_program(T, S, phases, dbg=False, middump=False):
    nc = bass.Bass("TRN2", target_bir_lowering=False, detect_race_conditions=False)
    es = contextlib.ExitStack()
    C = Ctx(nc, es, T, S)
    C.dbg_moe = dbg
    C.dbg_c = dbg
    dram = C.dram
    dram["x"] = nc.dram_tensor("x", [D, T], F32, kind="ExternalInput").ap()
    for name, shape in WEIGHT_SHAPES.items():
        dram[name] = nc.dram_tensor(name, shape, F32, kind="ExternalInput").ap()
    for name, shape in TABLE_SHAPES_BF.items():
        dram[name] = nc.dram_tensor(name, shape, BF16, kind="ExternalInput").ap()
    for name in ["tq_sin", "tq_cos", "tk_sin", "tk_cos"]:
        dram[name] = nc.dram_tensor(name, [D, S], BF16, kind="ExternalInput").ap()
    dram["xa"] = nc.dram_tensor("xa", [D, T], F32, kind="Internal").ap()
    dram["qx"] = nc.dram_tensor("qx", [D, T], BF16, kind="Internal").ap()
    dram["kx2"] = nc.dram_tensor("kx2", [T, NCH * 128], BF16, kind="Internal").ap()
    dram["xb"] = nc.dram_tensor("xb", [D, T], F32, kind="Internal").ap()
    dram["out"] = nc.dram_tensor("out", [D, T], F32, kind="ExternalOutput").ap()

    C.glob["cb"] = nc.alloc_sbuf_tensor("g_cb", [128, C_NCONST], F32)
    C.glob["ones"] = nc.alloc_sbuf_tensor("g_ones", [128, 128], BF16)
    C.glob["ident"] = nc.alloc_sbuf_tensor("g_ident", [128, 128], BF16)
    C.glob["rms_sb"] = nc.alloc_sbuf_tensor("g_rms", [128, NCH], F32)
    C.glob["split_sb"] = nc.alloc_sbuf_tensor("g_split", [128, NCH], F32)
    C.glob["bAout_sb"] = nc.alloc_sbuf_tensor("g_bAout", [128, NCH], F32)
    if dbg and ("G" in phases):
        C.glob["dbg_zz"] = nc.alloc_sbuf_tensor("g_dbg_zz", [128, 512], F32)
        C.glob["dbg_zsm"] = nc.alloc_sbuf_tensor("g_dbg_zsm", [128, 512], BF16)
        C.dram["dbg_zsm_out"] = nc.dram_tensor("dbg_zsm", [128, 512], BF16, kind="ExternalOutput").ap()
        dzz = nc.dram_tensor("dbg_zz", [128, 512], F32, kind="ExternalOutput").ap()
        C.dbg_g = True
        C.dram["dbg_zz_out"] = dzz
    if dbg and ("B" in phases or "D" in phases):
        C.dbg_b = True
        C.dram["dbg_weff"] = nc.dram_tensor("dbg_weff", [128, NCH * D], BF16, kind="ExternalOutput").ap()
        C.dram["dbg_wsc"] = nc.dram_tensor("dbg_wsc", [128, 4], F32, kind="ExternalOutput").ap()
        C.dram["dbg_wabs"] = nc.dram_tensor("dbg_wabs", [128, NCH], F32, kind="ExternalOutput").ap()
    if dbg and ("C" in phases):
        C.glob["dbg_ks"] = nc.alloc_sbuf_tensor("g_dbg_ks", [128, min(1024, S)], F32)
    if dbg and ("F" in phases or "H" in phases):
        _ckd = min(2048, T)
        C.glob["dbg_hf"] = nc.alloc_sbuf_tensor("g_dbg_hf", [128, NCH * _ckd], F32)
        C.glob["dbg_cm"] = nc.alloc_sbuf_tensor("g_dbg_cm", [128, _ckd], F32)

    with nc.Block() as block:
        C.block = block
        C.bar = C.sem("bar")
        C.st = C.sem("st")
        lg = C.sem("ldglob")

        def barrier_end():
            pass
        C.barrier_end = barrier_end

        @block.sync
        def _(sync):
            lg.inc(sync.dma_start(out=C.glob["cb"][:], in_=dram["cbank"]), 16)
            lg.inc(sync.dma_start(out=C.glob["ones"][:], in_=dram["ones_bf"]), 16)
            lg.inc(sync.dma_start(out=C.glob["ident"][:], in_=dram["ident_bf"]), 16)
            lg.inc(sync.dma_start(out=C.glob["rms_sb"][:], in_=dram["rms_c"]), 16)
            lg.inc(sync.dma_start(out=C.glob["split_sb"][:], in_=dram["split_c"]), 16)
            lg.inc(sync.dma_start(out=C.glob["bAout_sb"][:], in_=dram["bA_out"]), 16)

        chain = []
        cur = dram["x"]
        scratch = [dram["xa"], dram["xb"]]
        for i, ph in enumerate(phases):
            dstp = dram["out"] if i == len(phases) - 1 else scratch[i % 2]
            chain.append((ph, cur, dstp))
            cur = dstp
        for ph, s_, d_ in chain:
            {"A": emit_phase_attn_a, "B": emit_phase_ffn, "C": emit_phase_attn_b,
             "D": emit_phase_ffn, "E": emit_phase_retention, "F": emit_phase_moe,
             "G": emit_phase_attn_g, "H": emit_phase_moe}[ph](C, s_, d_)

        if middump:
            C.dram["dbg_mid"] = nc.dram_tensor("dbg_mid", [D, T], F32, kind="ExternalOutput").ap()
            bounce = nc.alloc_sbuf_tensor("dbg_bounce", [128, T], F32)
            dmid = C.sem("dmid")

            @block.sync
            def _(sync):
                sync.wait_ge(C.st.h, C.st.n)
                for c in range(NCH):
                    dmid.inc(sync.dma_start(out=bounce[:], in_=dram["xa"][128 * c:128 * c + 128, :]), 16)
                    dmid.wait(sync, 32 * c + 16)
                    dmid.inc(sync.dma_start(out=C.dram["dbg_mid"][128 * c:128 * c + 128, :],
                                            in_=bounce[:]), 16)
                    dmid.wait(sync, 32 * c + 32)

        @block.sync
        def _(sync):
            sync.wait_ge(C.st.h, C.st.n)
    return nc, C
